# version 09
# baseline (speedup 1.0000x reference)
import math
from contextlib import ExitStack

import numpy as np
import ml_dtypes

import concourse.bass as bass
import concourse.mybir as mybir
from concourse.bass_utils import run_bass_kernel_spmd

F32 = mybir.dt.float32
BF16 = mybir.dt.bfloat16
AF = mybir.ActivationFunctionType
ALU = mybir.AluOpType
AX = mybir.AxisListType
BF16NP = ml_dtypes.bfloat16

B, S, D = 2, 8192, 2048
H, DH = 16, 128
DFF = 1408
NDT = D // 128
NFT = DFF // 128
NCORES = 8
TPC = 2048
EPS = 1e-5
QSCALE = 1.0 / math.sqrt(DH)
NE = 18

ENGS = ("sync", "scalar", "vector", "tensor", "gpsimd")


class Sem:
    __slots__ = ("h", "n", "name")

    def __init__(self, h, name):
        self.h, self.n, self.name = h, 0, name


class Op:
    __slots__ = ("eng", "fn", "waits", "incs", "path")

    def __init__(self, eng, fn, waits, incs, path=None):
        self.eng, self.fn, self.waits, self.incs = eng, fn, waits, incs
        self.path = path


class Prog:

    def __init__(self, nc):
        self.nc = nc
        self.ops = []
        self._es = ExitStack()
        self.cur_path = None
        self.n_paths = 0

    def sem(self, name):
        return Sem(self.nc.alloc_semaphore(name=name), name)

    def sb(self, name, shape, dtype):
        return self._es.enter_context(self.nc.sbuf_tensor(name, shape, dtype))

    def ps(self, name, shape, dtype=F32):
        return self._es.enter_context(self.nc.psum_tensor(name, shape, dtype))

    def scope(self):
        return self._es.enter_context(ExitStack())

    def op(self, eng, fn, waits=(), inc=None, k=1):
        ws = [(w, w.n) if isinstance(w, Sem) else w for w in waits]
        incs = []
        if inc is not None:
            incs = [(inc, k)]
            inc.n += k
        self.ops.append(Op(eng, fn, ws, incs, self.cur_path))

    def set_path(self, j):
        self.cur_path = j
        if j is not None:
            self.n_paths = max(self.n_paths, j + 1)

    def dma(self, q, out, in_, sem, waits=(), transpose=False):
        self.op(q, lambda e, o=out, i=in_, t=transpose:
                e.dma_start(out=o, in_=i, transpose=t), waits, inc=sem, k=16)

    def fence(self, name, sems):
        f = self.sem(name)
        ws = [(s, s.n) for s in sems]
        for eng in ENGS:
            self.op(eng, lambda e, s=f: e.sem_inc(s.h, 1), waits=list(ws))
            f.n += 1
        for eng in ENGS:
            self.op(eng, lambda e: e.nop(), waits=[(f, f.n)])
        return f

    def emit(self):
        nc = self.nc
        prog = self

        with nc.Block() as block:
            for engname in ENGS:
                def make(engname=engname):
                    def run(eng):
                        jreg = None
                        if prog.n_paths:
                            pid = eng.partition_id()
                            jreg = eng.alloc_register("jreg_" + engname)
                            eng.reg_alu(jreg, pid, 4, ALU.mod)
                        seen = {}
                        seen_common = {}
                        cur_path = None
                        guard = None
                        for o in prog.ops:
                            if o.eng != engname:
                                continue
                            if o.path != cur_path:
                                if guard is not None:
                                    guard.__exit__(None, None, None)
                                    guard = None
                                if o.path is not None:
                                    guard = eng.If_eq(jreg, o.path)
                                    guard.__enter__()
                                    seen = dict(seen_common)
                                else:
                                    seen = seen_common
                                cur_path = o.path
                            for (s, v) in o.waits:
                                if v > 0 and seen.get(id(s), 0) < v:
                                    eng.wait_ge(s.h, v)
                                    seen[id(s)] = v
                            inst = o.fn(eng)
                            for (s, k) in o.incs:
                                inst.then_inc(s.h, k)
                        if guard is not None:
                            guard.__exit__(None, None, None)
                        if prog.n_paths:
                            eng.end_ifs()
                    return run
                getattr(block, engname)(make())

    def close(self):
        self._es.close()


def wslice_fm(w_ap, c0, cw):
    return w_ap[:, c0:c0 + cw].rearrange("(dt p) c -> p dt c", p=128)


def fm3(sb_ap, cw):
    return sb_ap.rearrange("p (dt c) -> p dt c", c=cw)


def emit_ln(p, x_ap, out_ap, stat, sq_scr, s_dve, s_act, eps_col, out2_ap=None):
    p.op("vector", lambda e: e.reduce_sum(stat[:, 0:1], x_ap, axis=AX.X),
         inc=s_dve)
    p.op("scalar", lambda e: e.activation(sq_scr, x_ap, AF.Square,
                                          accum_out=stat[:, 1:2]),
         waits=[s_dve], inc=s_act)
    p.op("vector", lambda e: e.tensor_scalar(stat[:, 2:3], stat[:, 0:1],
                                             1.0 / D, None, op0=ALU.mult),
         waits=[s_dve], inc=s_dve)
    p.op("vector", lambda e: e.tensor_tensor(stat[:, 3:4], stat[:, 2:3],
                                             stat[:, 2:3], op=ALU.mult),
         waits=[s_dve], inc=s_dve)
    p.op("vector", lambda e: e.tensor_scalar(stat[:, 4:5], stat[:, 1:2],
                                             1.0 / D, stat[:, 3:4],
                                             op0=ALU.mult, op1=ALU.subtract),
         waits=[s_dve, s_act], inc=s_dve)
    p.op("scalar", lambda e: e.activation(stat[:, 5:6], stat[:, 4:5], AF.Sqrt,
                                          bias=eps_col), waits=[s_dve],
         inc=s_act)
    p.op("vector", lambda e: e.reciprocal(stat[:, 6:7], stat[:, 5:6]),
         waits=[s_act], inc=s_dve)
    p.op("vector", lambda e: e.tensor_scalar(out_ap, x_ap, stat[:, 2:3],
                                             stat[:, 6:7], op0=ALU.subtract,
                                             op1=ALU.mult),
         waits=[s_dve], inc=s_dve)
    if out2_ap is not None:
        p.op("vector", lambda e: e.tensor_scalar(out2_ap, x_ap, stat[:, 2:3],
                                                 stat[:, 6:7],
                                                 op0=ALU.subtract,
                                                 op1=ALU.mult),
             waits=[s_dve], inc=s_dve)


def qkv_ctx(p, alloc, psalloc=None, idb=None):
    c = {}
    c["x_sb"] = [alloc(f"qx{i}", [128, D], F32) for i in range(2)]
    c["xn_sb"] = [alloc(f"qxn{i}", [128, D], BF16) for i in range(2)]
    c["stat"] = [alloc(f"qst{i}", [128, 8], F32) for i in range(2)]
    c["sq"] = alloc("qsq", [128, D], F32)
    c["eps"] = alloc("qeps", [128, 1], F32)
    c["xnT"] = [alloc(f"qxnT{i}", [128, NDT * 1024], BF16)
                for i in range(2)]
    c["wv_bf"] = alloc("qwv", [128, NDT * D], BF16)
    c["wst"] = [alloc(f"qwst{i}", [128, D], F32) for i in range(2)]
    c["wqk_bf"] = [alloc(f"qwqk{i}", [128, NDT * 128], BF16)
                   for i in range(2)]
    c["ev"] = [alloc(f"qev{i}", [128, 512], BF16) for i in range(2)]
    if psalloc is None:
        psalloc = p.ps
    c["mm_ps"] = [psalloc(f"qmm{i}", [128, 512]) for i in range(2)]
    c["tr_ps"] = [psalloc(f"qtr{i}", [128, 128], BF16) for i in range(2)]
    for k in ("x0", "x1", "dve", "act", "tr", "wld0", "wld1", "wv",
              "wqk0", "wqk1", "pe", "ev0", "ev1", "st0", "st1"):
        c["s_" + k] = p.sem("q_" + k)
    c["wquse"] = {}
    c["xnT_use"] = {}
    c["idb"] = idb
    p.op("vector", lambda e: e.memset(c["eps"][:, :], EPS), inc=c["s_dve"])
    return c


def emit_wv_cast(p, c, wvr):
    for dt in range(NDT):
        par = dt % 2
        p.dma("gpsimd", c["wst"][par][:, :], wvr[:, dt, :],
              c["s_wld%d" % par], waits=[c["s_wv"]])
        p.op("vector", lambda e, par=par, dt=dt:
             e.tensor_copy(c["wv_bf"][:, dt * D:(dt + 1) * D],
                           c["wst"][par][:, :]),
             waits=[c["s_wld%d" % par]], inc=c["s_wv"])


def emit_xnT(p, c, x_ap, ntok, xb_):
    s_dve, s_act, s_tr = c["s_dve"], c["s_act"], c["s_tr"]
    xnT = c["xnT"][xb_]
    for r in range(ntok // 128):
        par = r % 2
        p.dma("gpsimd", c["x_sb"][par][:, :],
              x_ap[r * 128:(r + 1) * 128, :], c["s_x%d" % par],
              waits=[s_dve, s_act, s_tr])
        p.op("vector", lambda e: e.nop(),
             waits=[c["s_x%d" % par], s_tr,
                    (c["s_pe"], c["xnT_use"].get(xb_, 0))])
        p.op("scalar", lambda e: e.nop(), waits=[c["s_x%d" % par]])
        emit_ln(p, c["x_sb"][par][:, :], c["xn_sb"][par][:, :],
                c["stat"][par][:, :], c["sq"][:, :], s_dve, s_act,
                c["eps"][:, :])
        if c["idb"] is not None:
            for dt in range(NDT):
                tp = (r * NDT + dt) % 2
                p.op("tensor", lambda e, par=par, dt=dt, tp=tp:
                     e.transpose(c["tr_ps"][tp][:, :],
                                 c["xn_sb"][par][:, dt * 128:(dt + 1) * 128],
                                 c["idb"][:, :]),
                     waits=[s_dve, (s_tr, c.setdefault("_trx", {})
                            .get(tp, 0)),
                            (c["s_pe"], c["xnT_use"].get(xb_, 0))],
                     inc=c["s_pe"])
                p.op("scalar", lambda e, dt=dt, r=r, tp=tp, xnT=xnT:
                     e.activation(xnT[:, dt * 1024 + r * 128:
                                      dt * 1024 + (r + 1) * 128],
                                  c["tr_ps"][tp][:, :], AF.Copy),
                     waits=[c["s_pe"]], inc=s_tr)
                c["_trx"][tp] = s_tr.n
        else:
            for dt in range(NDT):
                p.dma("sync",
                      xnT[:, dt * 1024 + r * 128: dt * 1024 + (r + 1) * 128],
                      c["xn_sb"][par][:, dt * 128:(dt + 1) * 128], s_tr,
                      waits=[s_dve, (c["s_pe"], c["xnT_use"].get(xb_, 0))],
                      transpose=True)


def emit_qk(p, c, wqkvr, cts, dst, row_base, col0, ntok, scale, xb_,
            bf16_src=False):
    s_pe = c["s_pe"]
    xnT = c["xnT"][xb_]
    for ct in cts:
        par = ct % 2
        if bf16_src:
            p.dma("gpsimd", c["wqk_bf"][par][:, :], wqkvr[ct],
                  c["s_wqk%d" % par],
                  waits=[(s_pe, c["wquse"].get(par, 0))])
        else:
            p.dma("gpsimd", fm3(c["wst"][par][:, :], 128), wqkvr[ct],
                  c["s_wld%d" % par], waits=[c["s_wqk%d" % par]])
            p.op("vector", lambda e, par=par:
                 e.tensor_copy(c["wqk_bf"][par][:, :], c["wst"][par][:, :]),
                 waits=[c["s_wld%d" % par],
                        (s_pe, c["wquse"].get(par, 0))],
                 inc=c["s_wqk%d" % par])
        for ts in range(ntok // 512):
            pp = (ct * 4 + ts) % 2
            for dt in range(NDT):
                p.op("tensor", lambda e, par=par, pp=pp, ts=ts, dt=dt:
                     e.matmul(c["mm_ps"][pp][:, :],
                              c["wqk_bf"][par][:, dt * 128:(dt + 1) * 128],
                              xnT[:, dt * 1024 + ts * 512:
                                  dt * 1024 + (ts + 1) * 512],
                              start=(dt == 0), stop=(dt == NDT - 1)),
                     waits=([c["s_wqk%d" % par], c["s_tr"],
                             c["s_ev%d" % pp]] if dt == 0 else ()),
                     inc=(s_pe if dt == NDT - 1 else None))
            row0 = ct * 128 - row_base
            p.op("scalar", lambda e, pp=pp, sc=scale:
                 e.activation(c["ev"][pp][:, :], c["mm_ps"][pp][:, :],
                              AF.Copy, scale=sc),
                 waits=[s_pe, c["s_st%d" % pp]], inc=c["s_ev%d" % pp])
            p.dma("scalar",
                  dst[row0:row0 + 128, col0 + ts * 512:col0 + (ts + 1) * 512],
                  c["ev"][pp][:, :], c["s_st%d" % pp],
                  waits=[c["s_ev%d" % pp]])
        c["wquse"][par] = s_pe.n
    c["xnT_use"][xb_] = s_pe.n


def emit_v(p, c, v_dst, row0_, ntok, xb_):
    s_pe = c["s_pe"]
    xnT = c["xnT"][xb_]
    for tt in range(ntok // 128):
        for vc in range(4):
            pp = (tt * 4 + vc) % 2
            for dt in range(NDT):
                p.op("tensor", lambda e, pp=pp, tt=tt, vc=vc, dt=dt:
                     e.matmul(c["mm_ps"][pp][:, :],
                              xnT[:, dt * 1024 + tt * 128:
                                  dt * 1024 + (tt + 1) * 128],
                              c["wv_bf"][:, dt * D + vc * 512:
                                         dt * D + (vc + 1) * 512],
                              start=(dt == 0), stop=(dt == NDT - 1)),
                     waits=([c["s_wv"], c["s_ev%d" % pp]] if dt == 0
                            else ()),
                     inc=(s_pe if dt == NDT - 1 else None))
            p.op("scalar", lambda e, pp=pp:
                 e.activation(c["ev"][pp][:, :], c["mm_ps"][pp][:, :],
                              AF.Copy),
                 waits=[s_pe, c["s_st%d" % pp]], inc=c["s_ev%d" % pp])
            p.dma("scalar",
                  v_dst[row0_ + tt * 128:row0_ + (tt + 1) * 128,
                        vc * 512:(vc + 1) * 512],
                  c["ev"][pp][:, :], c["s_st%d" % pp],
                  waits=[c["s_ev%d" % pp]])
    c["xnT_use"][xb_] = s_pe.n


def build_l1():
    nc = bass.Bass("TRN2", target_bir_lowering=False)
    x = nc.dram_tensor("x", [TPC, D], F32, kind="ExternalInput").ap()
    wqkvr = nc.dram_tensor("wqkvr", [2 * NDT, 128, NDT, 128], F32,
                           kind="ExternalInput").ap()
    wvr = nc.dram_tensor("wvr", [128, NDT, D], F32,
                         kind="ExternalInput").ap()
    qT = nc.dram_tensor("qT", [D, TPC], BF16, kind="ExternalOutput").ap()
    kT = nc.dram_tensor("kT", [D, TPC], BF16, kind="ExternalOutput").ap()
    v = nc.dram_tensor("v", [TPC, D], BF16, kind="ExternalOutput").ap()

    p = Prog(nc)
    c = qkv_ctx(p, p.sb)
    emit_wv_cast(p, c, wvr)
    for sb_ in range(2):
        xb_ = sb_ % 2
        emit_xnT(p, c, x[sb_ * 1024:(sb_ + 1) * 1024, :], 1024, xb_)
        emit_qk(p, c, wqkvr, list(range(NDT)), qT, 0, sb_ * 1024, 1024,
                QSCALE, xb_)
        emit_qk(p, c, wqkvr, list(range(NDT, 2 * NDT)), kT, D, sb_ * 1024,
                1024, 1.0, xb_)
        emit_v(p, c, v, sb_ * 1024, 1024, xb_)
    p.op("gpsimd", lambda e: e.nop(), waits=[c["s_st0"], c["s_st1"]])
    p.emit()
    p.close()
    return nc


def causal_mask_np():
    kk = np.arange(128)[:, None]
    qq = np.arange(512)[None, :]
    ms = [(kk + d <= qq).astype(BF16NP) for d in (0, 128, 256, 384)]
    return np.concatenate(ms, axis=1)


def emit_attn_pair(p, bufs, sems, q_src, k_src, v_src, o_dst, chunks,
                   qcols=None):
    if qcols is None:
        qcols = chunks
    (q_sb, k_sb, v_sb, pT, o_ev, l_row, masks_sb, st_ps, ot_ps, l_ps,
     lb_ps, lb_sb, onesc) = (
        bufs["q"], bufs["k"], bufs["v"], bufs["pT"], bufs["o_ev"],
        bufs["l_row"], bufs["masks"], bufs["st"], bufs["ot"], bufs["l"],
        bufs["lb"], bufs["lb_sb"], bufs["onesc"])
    l_rowf = bufs["l_rowf"]
    s_q, s_k, s_v, s_pe, s_exp, s_msk, s_dve, s_oev, s_act, s_ost = (
        sems["q"], sems["k"], sems["v"], sems["pe"], sems["exp"],
        sems["msk"], sems["dve"], sems["oev"], sems["act"], sems["ost"])

    nbuf = len(k_sb)
    st = bufs.setdefault("_state", {
        "kv_i": 0, "sp": 0, "op": 0, "kuse": {}, "quse": {}, "puse": {},
        "suse": {}, "otuse": {}, "evuse": {}, "luse": {},
        "lbuse": [0, 0]})
    q_i = [0]

    def load_block(bi, kb):
        kp = bi % nbuf
        kw = [(s_pe, st["kuse"].get(kp, 0))]
        p.dma("gpsimd", k_sb[kp][:, :],
              k_src[:, kb * 512:(kb + 1) * 512], s_k[kp], waits=kw)
        for t4 in range(4):
            p.dma("scalar", v_sb[kp][:, t4 * 128:(t4 + 1) * 128],
                  v_src[kb * 512 + t4 * 128: kb * 512 + (t4 + 1) * 128, :],
                  s_v[kp], waits=kw)

    def load_q(ci):
        if ci >= len(chunks):
            return
        qp = ci % 3
        p.dma("gpsimd", q_sb[qp][:, :],
              q_src[:, qcols[ci] * 512:(qcols[ci] + 1) * 512], s_q[qp],
              waits=[(s_pe, st["quse"].get(qp, 0))])

    load_q(0)
    load_q(1)
    q_i[0] = 2

    for ci, qs in enumerate(chunks):
        qp = ci % 3
        opar = st["op"] % 2
        st["op"] += 1
        work = []
        for kb in range(qs + 1):
            for kt in range(4):
                delta = (kb * 512 + kt * 128) - qs * 512
                if delta < 512:
                    work.append((kb, kt, delta))
        starts = {}
        for wi, (kb, kt, _) in enumerate(work):
            if kb not in starts:
                starts[kb] = wi
        issue_at = {}
        for kb, w0 in sorted(starts.items()):
            issue_at.setdefault(max(0, w0 - 8), []).append(kb)
        bis = {}

        def s_tile(wi):
            kb, kt, delta = work[wi]
            for kb2 in issue_at.get(wi, ()):
                bi = st["kv_i"]
                st["kv_i"] += 1
                bis[kb2] = bi
                load_block(bi, kb2)
            kp = bis[kb] % nbuf
            sp = st["sp"] % 3
            st["sp"] += 1
            p.op("tensor", lambda e, sp=sp, kp=kp, qp=qp, kt=kt:
                 e.matmul(st_ps[sp][:, :],
                          k_sb[kp][:, kt * 128:(kt + 1) * 128],
                          q_sb[qp][:, :], start=True, stop=True),
                 waits=[s_k[kp], s_q[qp], (s_exp, st["suse"].get(sp, 0))],
                 inc=s_pe)
            ppe, pmsk = st["puse"].get(sp, (0, 0))
            p.op("scalar", lambda e, sp=sp:
                 e.activation(pT[sp][:, :], st_ps[sp][:, :], AF.Exp),
                 waits=[s_pe, (s_pe, ppe), (s_msk, pmsk)], inc=s_exp)
            st["suse"][sp] = s_exp.n
            if delta >= 0:
                mi = delta // 128
                p.op("vector", lambda e, sp=sp, mi=mi:
                     e.tensor_tensor(pT[sp][:, :], pT[sp][:, :],
                                     masks_sb[:, mi * 512:(mi + 1) * 512],
                                     op=ALU.mult),
                     waits=[s_exp], inc=s_msk)
            return (sp, s_exp.n, s_msk.n)

        def av_tile(wi, spt, first, last):
            sp, expn, mskn = spt
            kb, kt, delta = work[wi]
            kp = bis[kb] % nbuf
            pw = ([(s_msk, mskn)] if delta >= 0 else [(s_exp, expn)])
            fw = ([(s_oev, st["otuse"].get(opar, 0)),
                   (s_dve, st["luse"].get(opar, 0))] if first else [])
            p.op("tensor", lambda e, sp=sp, kp=kp, kt=kt, first=first,
                 last=last, opar=opar:
                 e.matmul(ot_ps[opar][:, :],
                          v_sb[kp][:, kt * 128:(kt + 1) * 128],
                          pT[sp][:, :], start=first, stop=last),
                 waits=(pw + [s_v[kp]] + fw),
                 inc=(s_pe if last else None))
            p.op("tensor", lambda e, sp=sp, first=first, last=last,
                 opar=opar:
                 e.matmul(l_ps[opar][:, :], onesc[:, 0:1], pT[sp][:, :],
                          start=first, stop=last),
                 inc=(s_pe if last else None))
            st["puse"][sp] = (s_pe.n, s_msk.n)
            st["kuse"][kp] = s_pe.n

        n = len(work)
        pend = [s_tile(0)]
        if n > 1:
            pend.append(s_tile(1))
        for wi in range(2, n):
            pend.append(s_tile(wi))
            spt = pend.pop(0)
            av_tile(wi - 2, spt, first=(wi - 2 == 0), last=False)
            if wi == n // 2:
                load_q(q_i[0])
                q_i[0] += 1
        for k2, spt in enumerate(pend):
            wi2 = n - len(pend) + k2
            av_tile(wi2, spt, first=(wi2 == 0), last=(wi2 == n - 1))
        if n <= 2:
            load_q(q_i[0])
            q_i[0] += 1
        st["quse"][qp] = s_pe.n

        p.op("vector", lambda e, opar=opar:
             e.reciprocal(l_rowf[:, :], l_ps[opar][0:1, :]),
             waits=[s_pe, (s_oev, st["lbuse"][0])], inc=s_dve)
        p.op("vector", lambda e:
             e.tensor_copy(l_row[:, :], l_rowf[:, :]),
             waits=[s_dve], inc=s_dve)
        st["luse"][opar] = s_dve.n
        p.op("tensor", lambda e, opar=opar:
             e.matmul(lb_ps[:, :], onesc[0:1, 0:128], l_row[:, :],
                      start=True, stop=True),
             waits=[s_dve, (s_act, st["lbuse"][1])], inc=s_pe)
        p.op("scalar", lambda e:
             e.activation(lb_sb[:, :], lb_ps[:, :], AF.Copy),
             waits=[s_pe, (s_dve, s_dve.n)], inc=s_act)
        st["lbuse"][1] = s_act.n
        op2 = st["op"] % 2
        p.op("vector", lambda e, opar=opar, op2=op2:
             e.tensor_tensor(o_ev[op2][:, :], ot_ps[opar][:, :],
                             lb_sb[:, :], op=ALU.mult),
             waits=[s_act, (s_ost, st["evuse"].get(op2, 0))], inc=s_oev)
        st["otuse"][opar] = s_oev.n
        st["lbuse"][0] = s_oev.n
        p.dma("sync", o_dst[:, ci * 512:(ci + 1) * 512], o_ev[op2][:, :],
              s_ost, waits=[s_oev])
        st["evuse"][op2] = s_ost.n


def build_l2():
    nc = bass.Bass("TRN2", target_bir_lowering=False)
    qs_ = [nc.dram_tensor(f"q{i}", [128, S], BF16, kind="ExternalInput").ap()
           for i in range(4)]
    ks_ = [nc.dram_tensor(f"k{i}", [128, S], BF16, kind="ExternalInput").ap()
           for i in range(4)]
    vs_ = [nc.dram_tensor(f"v{i}", [S, 128], BF16, kind="ExternalInput").ap()
           for i in range(4)]
    os_ = [nc.dram_tensor(f"o{i}", [128, S], BF16, kind="ExternalOutput").ap()
           for i in range(4)]
    masks = nc.inline_tensor(causal_mask_np(), name="cmasks")

    p = Prog(nc)
    bufs = {
        "q": [p.sb(f"qb{i}", [128, 512], BF16) for i in range(3)],
        "k": [p.sb(f"kb{i}", [128, 512], BF16) for i in range(4)],
        "v": [p.sb(f"vb{i}", [128, 4 * 128], BF16) for i in range(4)],
        "pT": [p.sb(f"pT{i}", [128, 512], BF16) for i in range(3)],
        "o_ev": [p.sb(f"oe{i}", [128, 512], BF16) for i in range(2)],
        "l_row": p.sb("l_row", [1, 512], BF16),
        "l_rowf": p.sb("l_rowf", [1, 512], F32),
        "lb_sb": p.sb("lb_sb", [128, 512], BF16),
        "onesc": p.sb("onesc", [128, 128], BF16),
        "masks": p.sb("masks", [128, 4 * 512], BF16),
        "st": [p.ps(f"st{i}", [128, 512]) for i in range(3)],
        "ot": [p.ps(f"ot{i}", [128, 512]) for i in range(2)],
        "l": [p.ps(f"l{i}", [1, 512]) for i in range(2)],
        "lb": p.ps("lb", [128, 512], F32),
    }
    sems = {k: p.sem("s_" + k) for k in
            ("pe", "exp", "msk", "dve", "oev", "act", "ost", "ld")}
    sems["q"] = [p.sem(f"s_q{i}") for i in range(3)]
    sems["k"] = [p.sem(f"s_k{i}") for i in range(4)]
    sems["v"] = [p.sem(f"s_v{i}") for i in range(4)]

    p.dma("gpsimd", bufs["masks"][:, :], masks.ap(), sems["ld"])
    p.op("vector", lambda e: e.memset(bufs["onesc"][:, :], 1.0),
         inc=sems["msk"])
    p.op("vector", lambda e: e.nop(), waits=[sems["ld"]])
    p.op("tensor", lambda e: e.nop(), waits=[sems["msk"], sems["ld"]])

    for pair in range(4):
        emit_attn_pair(p, bufs, sems, qs_[pair], ks_[pair], vs_[pair],
                       os_[pair], chunks=list(range(S // 512)))

    p.op("gpsimd", lambda e: e.nop(), waits=[sems["ost"]])
    p.emit()
    p.close()
    return nc


def emit_l3_body(p, nc, x, oT, woutr, sw1, sb1, sw2, sb2, rw1, rb1, rw2,
                 rb2, cent, cbias, y, x1_scr, uT_scr, ident_bf, ident_f,
                 ones_r, ones_rb):

    idb = p.sb("idb", [128, 128], BF16)
    idf = p.sb("idf", [128, 128], F32)
    ones_sb = p.sb("ones", [1, 128], F32)
    ones_bsb = p.sb("onesb", [1, 128], BF16)
    gT_bf = p.sb("gTb", [16, TPC], BF16)
    b1T = p.sb("b1T", [128, NFT * NE], F32)
    b2r_bf = p.sb("b2r", [16, D], BF16)
    b2s_bf = p.sb("b2s", [2, D], BF16)
    onesK2 = p.sb("onesK2", [2, 512], BF16)
    stat = [p.sb(f"st{i}", [128, 8], F32) for i in range(4)]
    sq_scr = p.sb("sq", [128, D], BF16)
    eps_col = p.sb("eps", [128, 1], F32)

    s = {k: p.sem("s_" + k) for k in
         ("ld", "pe", "act", "dve", "pool", "tr", "st", "prm")}

    centT = p.sb("centT", [128, NDT * 16], BF16)
    bias_pl = p.sb("biaspl", [128, 16], F32)
    scP = p.scope()
    prm = scP.enter_context(nc.sbuf_tensor("prm", [16, 2 * D + DFF], F32))
    prm2 = scP.enter_context(nc.sbuf_tensor("prm2", [2, D + DFF], F32))
    cent_bf = scP.enter_context(nc.sbuf_tensor("centb", [16, D], BF16))
    bias_row = scP.enter_context(nc.sbuf_tensor("biasrow", [1, 16], F32))
    tp_ps = scP.enter_context(nc.psum_tensor("tp_ps", [128, 128], F32))
    tpb_ps = scP.enter_context(nc.psum_tensor("tpb_ps", [128, 128], BF16))

    p.op("vector", lambda e: e.memset(eps_col[:, :], EPS), inc=s["dve"])
    p.dma("gpsimd", idb[:, :], ident_bf.ap(), s["ld"])
    p.dma("gpsimd", idf[:, :], ident_f.ap(), s["ld"])
    p.dma("gpsimd", ones_sb[:, :], ones_r.ap(), s["ld"])
    p.dma("gpsimd", ones_bsb[:, :], ones_rb.ap(), s["ld"])
    p.dma("gpsimd", prm[:, 0:D], cent, s["ld"])
    p.dma("gpsimd", prm[:, D:2 * D], rb2, s["ld"])
    p.dma("gpsimd", prm[:, 2 * D:2 * D + DFF], rb1, s["ld"])
    p.dma("gpsimd", prm2[:, 0:D], sb2, s["ld"])
    p.dma("gpsimd", prm2[:, D:D + DFF], sb1, s["ld"])
    p.dma("gpsimd", bias_row[:, :], cbias, s["ld"])
    p.op("vector", lambda e: e.tensor_copy(cent_bf[:, :], prm[:, 0:D]),
         waits=[s["ld"]], inc=s["dve"])
    p.op("vector", lambda e: e.tensor_copy(b2r_bf[:, :], prm[:, D:2 * D]),
         waits=[s["ld"]], inc=s["dve"])
    p.op("vector", lambda e: e.tensor_copy(b2s_bf[:, :], prm2[:, 0:D]),
         waits=[s["ld"]], inc=s["dve"])
    p.op("vector", lambda e: e.memset(onesK2[:, :], 1.0), inc=s["dve"])
    for dt in range(NDT):
        p.op("tensor", lambda e, dt=dt:
             e.transpose(tpb_ps[:, 0:16], cent_bf[:, dt * 128:(dt + 1) * 128],
                         idb[0:16, 0:16]),
             waits=[s["dve"], s["ld"], s["act"]], inc=s["pe"])
        p.op("scalar", lambda e, dt=dt:
             e.activation(centT[:, dt * 16:(dt + 1) * 16], tpb_ps[:, 0:16],
                          AF.Copy), waits=[s["pe"]], inc=s["act"])
    for f in range(NFT):
        p.op("tensor", lambda e, f=f:
             e.transpose(tp_ps[:, 0:16], prm[:, 2 * D + f * 128:
                                             2 * D + (f + 1) * 128],
                         idf[0:16, 0:16]),
             waits=[s["ld"], s["act"]], inc=s["pe"])
        p.op("scalar", lambda e, f=f:
             e.activation(b1T[:, f * NE:f * NE + 16], tp_ps[:, 0:16],
                          AF.Copy), waits=[s["pe"]], inc=s["act"])
        p.op("tensor", lambda e, f=f:
             e.transpose(tp_ps[:, 0:2], prm2[:, D + f * 128:
                                             D + (f + 1) * 128],
                         idf[0:2, 0:2]),
             waits=[s["ld"], s["act"]], inc=s["pe"])
        p.op("scalar", lambda e, f=f:
             e.activation(b1T[:, f * NE + 16:f * NE + 18], tp_ps[:, 0:2],
                          AF.Copy), waits=[s["pe"]], inc=s["act"])
    p.op("tensor", lambda e:
         e.matmul(tp_ps[:, 0:16], ones_sb[:, :], bias_row[:, :],
                  start=True, stop=True),
         waits=[s["ld"], s["act"]], inc=s["pe"])
    p.op("scalar", lambda e:
         e.activation(bias_pl[:, :], tp_ps[:, 0:16], AF.Copy),
         waits=[s["pe"]], inc=s["act"])
    p.fence("fP", [s["pe"], s["act"], s["dve"], s["ld"]])
    scP.close()

    scA = p.scope()

    def sbA(name, shape, dtype):
        return scA.enter_context(nc.sbuf_tensor(name, shape, dtype))

    oT_sb = [sbA(f"oTs{i}", [128, NDT * 512], BF16) for i in range(2)]
    wo_st = [sbA(f"wost{i}", [128, D], F32) for i in range(2)]
    wo_bf = [sbA(f"wob{i}", [128, D], BF16) for i in range(2)]
    aoev = [sbA(f"aoev{i}", [128, 512], F32) for i in range(2)]
    ao_tm = [sbA(f"aotm{i}", [128, D], F32) for i in range(8)]
    a_t = sbA("a_t", [128, D], F32)
    x_t = [sbA(f"x_t{i}", [128, D], F32) for i in range(2)]
    x1_t = sbA("x1_t", [128, D], F32)
    u_bf = sbA("u_bf", [128, D], BF16)
    uT_tile = [sbA(f"uTt{i}", [128, NDT * 128], BF16) for i in range(2)]
    aops = scA.enter_context(nc.psum_tensor("aops", [128, 512], F32))
    aops2 = scA.enter_context(nc.psum_tensor("aops2", [128, 512], F32))
    trps = [scA.enter_context(nc.psum_tensor(f"trps{i}", [128, 128], F32))
            for i in range(2)]
    sA = {k: p.sem("sA_" + k) for k in
          ("oT", "wld", "wc", "pe", "ev", "tr", "trev", "xld", "dve", "act",
           "ubf", "utr", "str", "ustr")}

    wouse, aoev_use, trps_use, aotm_use = {}, {}, {}, {}
    ustr_tile = {}

    def a_mm_block(ts):

        otp = ts % 2
        p.dma("gpsimd", fm3(oT_sb[otp][:, :], 512),
              oT[:, ts * 512:(ts + 1) * 512]
              .rearrange("(dt p) t -> p dt t", p=128),
              sA["oT"], waits=[sA["pe"]])
        for dto in range(NDT):
            wp = dto % 2
            psA_ = aops if (dto % 2 == 0) else aops2
            p.dma("gpsimd", fm3(wo_st[wp][:, :], 128),
                  woutr[dto], sA["wld"], waits=[sA["wc"]])
            p.op("vector", lambda e, wp=wp:
                 e.tensor_copy(wo_bf[wp][:, :], wo_st[wp][:, :]),
                 waits=[sA["wld"], (sA["pe"], wouse.get(wp, 0))],
                 inc=sA["wc"])
            for dt in range(NDT):
                p.op("tensor", lambda e, psA_=psA_, wp=wp, otp=otp, dt=dt:
                     e.matmul(psA_[:, :], wo_bf[wp][:, dt * 128:(dt + 1) * 128],
                              oT_sb[otp][:, dt * 512:(dt + 1) * 512],
                              start=(dt == 0), stop=(dt == NDT - 1)),
                     waits=([sA["wc"], sA["oT"], sA["ev"]] if dt == 0 else ()),
                     inc=(sA["pe"] if dt == NDT - 1 else None))
            wouse[wp] = sA["pe"].n
            ep = dto % 2
            p.op("scalar", lambda e, psA_=psA_, ep=ep:
                 e.activation(aoev[ep][:, :], psA_[:, :], AF.Copy),
                 waits=[sA["pe"], (sA["tr"], aoev_use.get(ep, 0))],
                 inc=sA["ev"])
            for tt in range(4):
                tp = (dto * 4 + tt) % 2
                p.op("tensor", lambda e, ep=ep, tt=tt, tp=tp:
                     e.transpose(trps[tp][:, :],
                                 aoev[ep][:, tt * 128:(tt + 1) * 128],
                                 idf[:, :]),
                     waits=[sA["ev"], (sA["trev"], trps_use.get(tp, 0))],
                     inc=sA["tr"])
                p.op("scalar", lambda e, ts=ts, tt=tt, tp=tp, dto=dto:
                     e.activation(ao_tm[(ts % 2) * 4 + tt]
                                  [:, dto * 128:(dto + 1) * 128],
                                  trps[tp][:, :], AF.Copy),
                     waits=[sA["tr"],
                            (sA["dve"],
                             aotm_use.get((ts % 2) * 4 + tt, 0))],
                     inc=sA["trev"])
                trps_use[tp] = sA["trev"].n
            aoev_use[ep] = sA["tr"].n

    def a_ln_block(ts):
        for tt in range(4):
            r0 = ts * 512 + tt * 128
            xp = tt % 2
            p.dma("gpsimd", x_t[xp][:, :], x[r0:r0 + 128, :], sA["xld"],
                  waits=[sA["dve"]])
            p.op("vector", lambda e: e.nop(), waits=[sA["trev"], sA["xld"]])
            p.op("scalar", lambda e: e.nop(), waits=[sA["trev"]])
            emit_ln(p, ao_tm[(ts % 2) * 4 + tt][:, :], a_t[:, :], stat[tt % 2][:, :],
                    sq_scr[:, :], sA["dve"], sA["act"], eps_col[:, :])
            aotm_use[(ts % 2) * 4 + tt] = sA["dve"].n
            p.op("vector", lambda e, xp=xp:
                 e.tensor_tensor(x1_t[:, :], x_t[xp][:, :], a_t[:, :],
                                 op=ALU.add),
                 waits=[sA["dve"], sA["str"], sA["act"]], inc=sA["dve"])
            p.dma("scalar", x1_scr[r0:r0 + 128, :], x1_t[:, :], sA["str"],
                  waits=[sA["dve"]])
            p.op("scalar", lambda e: e.nop(), waits=[sA["dve"]])
            p.op("vector", lambda e: e.nop(), waits=[sA["utr"], sA["str"]])
            emit_ln(p, x1_t[:, :], u_bf[:, :], stat[2 + tt % 2][:, :],
                    sq_scr[:, :], sA["dve"], sA["act"], eps_col[:, :])
            up = (ts * 4 + tt) % 2
            for dt in range(NDT):
                p.dma("sync",
                      uT_tile[up][:, dt * 128:(dt + 1) * 128],
                      u_bf[:, dt * 128:(dt + 1) * 128], sA["utr"],
                      waits=([sA["dve"], sA["ustr"]] if dt == 0
                             else [sA["dve"]]), transpose=True)
            p.dma("scalar",
                  uT_scr[:, r0:r0 + 128]
                  .rearrange("(dt p) t -> p dt t", p=128),
                  fm3(uT_tile[up][:, :], 128), sA["ustr"],
                  waits=[sA["utr"]])
            ustr_tile[r0 // 128] = sA["ustr"].n


    for ts in range(4):
        a_mm_block(ts)
        if ts >= 1:
            a_ln_block(ts - 1)
    a_ln_block(3)
    a_final = {k: (sA[k], sA[k].n) for k in
               ("pe", "ev", "tr", "trev", "dve", "act", "str", "ustr")}
    scA.close()

    scB = p.scope()

    def sbB(name, shape, dtype):
        return scB.enter_context(nc.sbuf_tensor(name, shape, dtype))

    uTt = [sbB(f"uTb{i}", [128, NDT * 128], BF16) for i in range(2)]
    s0 = sbB("s0", [128, 16], F32)
    sa = sbB("sa", [128, 16], F32)
    sb_ = sbB("sb", [128, 16], F32)
    t16 = sbB("t16", [128, 16], F32)
    g16 = sbB("g16", [128, 16], F32)
    gst = sbB("gst", [128, 8], F32)
    scps = scB.enter_context(nc.psum_tensor("scps", [128, 16], F32))
    gtps = scB.enter_context(nc.psum_tensor("gtps", [128, 128], F32))
    sB = {k: p.sem("sB_" + k) for k in ("pe", "act", "dve", "tr", "uld")}

    for tt in range(TPC // 128):
        up = tt % 2
        p.dma("gpsimd", fm3(uTt[up][:, :], 128),
              uT_scr[:, tt * 128:(tt + 1) * 128]
              .rearrange("(dt p) t -> p dt t", p=128),
              sB["uld"], waits=[sB["pe"],
                                (sA["ustr"], ustr_tile[tt])])
        for dt in range(NDT):
            p.op("tensor", lambda e, up=up, dt=dt:
                 e.matmul(scps[:, :],
                          uTt[up][:, dt * 128:(dt + 1) * 128],
                          centT[:, dt * 16:(dt + 1) * 16],
                          start=(dt == 0), stop=(dt == NDT - 1)),
                 waits=([sB["act"], sB["uld"]] + ([a_final["ev"],
                                                   a_final["trev"],
                                                   a_final["tr"]]
                                                  if tt == 0 else [])
                        if dt == 0 else ()),
                 inc=(sB["pe"] if dt == NDT - 1 else None))
        p.op("scalar", lambda e:
             e.activation(sa[:, :], scps[:, :], AF.Sigmoid),
             waits=[sB["pe"], sB["dve"]] + ([a_final["act"],
                                             a_final["dve"]]
                                            if tt == 0 else []),
             inc=sB["act"])
        p.op("vector", lambda e:
             e.tensor_tensor(s0[:, :], sa[:, :], bias_pl[:, :], op=ALU.add),
             waits=[sB["act"], sB["tr"]], inc=sB["dve"])
        cur = s0
        for r in range(4):
            p.op("vector", lambda e, cur=cur, r=r:
                 e.reduce_max(gst[:, r:r + 1], cur[:, :], axis=AX.X),
                 waits=[sB["dve"]], inc=sB["dve"])
            if r < 3:
                nxt = sb_ if cur is not sb_ else sa
                p.op("vector", lambda e, cur=cur, r=r:
                     e.tensor_scalar(t16[:, :], cur[:, :], gst[:, r:r + 1],
                                     None, op0=ALU.is_ge),
                     waits=[sB["dve"]], inc=sB["dve"])
                p.op("vector", lambda e, cur=cur, nxt=nxt:
                     e.scalar_tensor_tensor(nxt[:, :], t16[:, :], -1e30,
                                            cur[:, :], op0=ALU.mult,
                                            op1=ALU.add),
                     waits=[sB["dve"]], inc=sB["dve"])
                cur = nxt
        p.op("vector", lambda e:
             e.tensor_scalar(t16[:, :], s0[:, :], gst[:, 3:4], None,
                             op0=ALU.is_ge), waits=[sB["dve"]], inc=sB["dve"])
        p.op("vector", lambda e:
             e.tensor_tensor(g16[:, :], s0[:, :], t16[:, :], op=ALU.mult),
             waits=[sB["dve"]], inc=sB["dve"])
        p.op("vector", lambda e:
             e.reduce_sum(gst[:, 4:5], g16[:, :], axis=AX.X),
             waits=[sB["dve"]], inc=sB["dve"])
        p.op("vector", lambda e:
             e.tensor_scalar(gst[:, 5:6], gst[:, 4:5], 1e-8, None,
                             op0=ALU.max), waits=[sB["dve"]], inc=sB["dve"])
        p.op("vector", lambda e:
             e.reciprocal(gst[:, 6:7], gst[:, 5:6]),
             waits=[sB["dve"]], inc=sB["dve"])
        p.op("vector", lambda e:
             e.tensor_scalar(g16[:, :], g16[:, :], gst[:, 6:7], None,
                             op0=ALU.mult), waits=[sB["dve"]], inc=sB["dve"])
        p.op("tensor", lambda e:
             e.transpose(gtps[0:16, 0:128], g16[:, :], idf[:, :]),
             waits=[sB["dve"], sB["act"]], inc=sB["tr"])
        p.op("scalar", lambda e, tt=tt:
             e.activation(gT_bf[:, tt * 128:(tt + 1) * 128],
                          gtps[0:16, 0:128], AF.Copy),
             waits=[sB["tr"]], inc=sB["act"])

    p.fence("fB", [sB[k] for k in sB])
    scB.close()

    acc = p.sb("acc", [128, NDT * 1024], F32)

    y_st = [p.sem(f"sy{i}") for i in range(2)]
    sems_C = {k: p.sem("sC_" + k) for k in
              ("w1ld", "w1c", "w2ld", "w2c", "pe", "relu", "gate", "accadd",
               "grow", "gpl", "uld")}
    sems_F = {k: p.sem("sF_" + k) for k in
              ("tr", "trev", "xld", "dve", "act")}
    for half in range(2):
        scW = p.scope()

        def sbW(name, shape, dtype):
            return scW.enter_context(nc.sbuf_tensor(name + str(half),
                                                    shape, dtype))

        uTh = sbW("uTh", [128, NDT * 1024], BF16)
        h_all = sbW("h_all", [128, NFT * 1024], BF16)
        w1_st = [sbW("w1st", [128, D], F32)] * 2
        w1_bf = [sbW(f"w1bf{i}", [128, D], BF16) for i in range(2)]
        w2_st = [sbW("w2st", [128, NFT * 512], F32)] * 2
        w2_bf = [sbW(f"w2bf{i}", [128, NFT * 512], BF16) for i in range(2)]
        g_pl = sbW("g_pl", [128, 1024], BF16)
        grow = sbW("grow", [1, 1024], BF16)
        h_ps = [scW.enter_context(nc.psum_tensor(f"hps{i}{half}",
                                                 [128, 1024], F32))
                for i in range(2)]
        y_ps = [scW.enter_context(nc.psum_tensor(f"yps{i}{half}",
                                                 [128, 512], F32))
                for i in range(4)]
        sC = sems_C

        w1use, w2use, yuse = {}, {}, {}
        p.dma("gpsimd", fm3(uTh[:, :], 1024),
              uT_scr[:, half * 1024:(half + 1) * 1024]
              .rearrange("(dt p) t -> p dt t", p=128), sC["uld"])
        w1_jobs = []
        w2_jobs = []
        for e_i in range(NE):
            w1_jobs += [(e_i, f) for f in range(NFT)]
            w2_jobs += [(e_i, dg) for dg in range(4)]

        def w1_fetch(i):
            if i >= len(w1_jobs):
                return
            e_j, f_j = w1_jobs[i]
            wsrc = (rw1[e_j] if e_j < 16 else sw1[e_j - 16])
            wp = i % 2
            st3 = fm3(w1_st[0][:, :], 128)
            p.dma("gpsimd", st3[:, 0:NDT // 2, :], wsrc[f_j][:, 0:NDT // 2, :],
                  sC["w1ld"], waits=[sC["w1c"]])
            p.dma("scalar", st3[:, NDT // 2:, :], wsrc[f_j][:, NDT // 2:, :],
                  sC["w1ld"], waits=[sC["w1c"]])
            p.op("vector", lambda e, wp=wp:
                 e.tensor_copy(w1_bf[wp][:, :], w1_st[0][:, :]),
                 waits=[sC["w1ld"], (sC["pe"], w1use.get(wp, 0))],
                 inc=sC["w1c"])

        def w2_fetch(i):
            if i >= len(w2_jobs):
                return
            e_j, dg_j = w2_jobs[i]
            wsrc = (rw2[e_j] if e_j < 16 else sw2[e_j - 16])
            wp2 = i % 2
            st3 = fm3(w2_st[0][:, :], 512)
            p.dma("gpsimd", st3[:, 0:6, :], wsrc[dg_j][:, 0:6, :],
                  sC["w2ld"], waits=[sC["w2c"]])
            p.dma("scalar", st3[:, 6:, :], wsrc[dg_j][:, 6:, :],
                  sC["w2ld"], waits=[sC["w2c"]])
            p.op("vector", lambda e, wp2=wp2:
                 e.tensor_copy(w2_bf[wp2][:, :], w2_st[0][:, :]),
                 waits=[sC["w2ld"], (sC["pe"], w2use.get(wp2, 0))],
                 inc=sC["w2c"])

        w1_fetch(0)
        w2_fetch(0)
        w1c_counts = {}
        w2c_counts = {}
        w1c_counts[0] = sC["w1c"].n
        w2c_counts[0] = sC["w2c"].n
        w1_i = [1]
        w2_i = [1]

        for e_i in range(NE):
            routed = e_i < 16
            hp = e_i % 2
            if routed:
                p.dma("sync", grow[:, :],
                      gT_bf[e_i:e_i + 1, half * 1024:(half + 1) * 1024],
                      sC["grow"], waits=[sC["gpl"]])
                for q2 in range(2):
                    p.op("tensor", lambda e, hp=hp, q2=q2:
                         e.matmul(h_ps[hp][:, q2 * 512:(q2 + 1) * 512],
                                  ones_bsb[:, :],
                                  grow[:, q2 * 512:(q2 + 1) * 512],
                                  start=True, stop=True),
                         waits=([sC["grow"], sC["relu"], sC["gate"]]
                                if q2 == 0 else ()),
                         inc=(sC["pe"] if q2 == 1 else None))
                p.op("scalar", lambda e, hp=hp:
                     e.activation(g_pl[:, :], h_ps[hp][:, :], AF.Copy),
                     waits=[sC["pe"], sC["gate"]], inc=sC["gpl"])
            for f in range(NFT):
                wi = e_i * NFT + f
                wp = wi % 2
                w1_fetch(w1_i[0])
                w1c_counts[w1_i[0]] = sC["w1c"].n
                w1_i[0] += 1
                for q2 in range(2):
                    for dt in range(NDT):
                        p.op("tensor", lambda e, hp=hp, wp=wp, q2=q2, dt=dt:
                             e.matmul(h_ps[hp][:, q2 * 512:(q2 + 1) * 512],
                                      w1_bf[wp][:, dt * 128:(dt + 1) * 128],
                                      uTh[:, dt * 1024 + q2 * 512:
                                          dt * 1024 + (q2 + 1) * 512],
                                      start=(dt == 0), stop=(dt == NDT - 1)),
                             waits=([(sC["w1c"], w1c_counts[wi]),
                                     sC["relu"], sC["gate"],
                                     sC["gpl"], sC["uld"]] if dt == 0
                                    else ()),
                             inc=(sC["pe"] if dt == NDT - 1 else None))
                w1use[wp] = sC["pe"].n
                bcol = b1T[:, f * NE + e_i:f * NE + e_i + 1]
                hsl = h_all[:, f * 1024:(f + 1) * 1024]
                p.op("scalar", lambda e, hsl=hsl, hp=hp, bcol=bcol:
                     e.activation(hsl, h_ps[hp][:, :], AF.Relu, bias=bcol),
                     waits=[sC["pe"], sC["gate"]], inc=sC["relu"])
                if routed:
                    p.op("vector", lambda e, hsl=hsl:
                         e.tensor_tensor(hsl, hsl, g_pl[:, :], op=ALU.mult),
                         waits=[sC["relu"], sC["gpl"]], inc=sC["gate"])
            for dg in range(4):
                wi2 = e_i * 4 + dg
                wp2 = wi2 % 2
                w2_fetch(w2_i[0])
                w2c_counts[w2_i[0]] = sC["w2c"].n
                w2_i[0] += 1
                for q2 in range(2):
                    for dh in range(2):
                        yset = (wi2 * 4 + q2 * 2 + dh) % 2
                        ybase = yset * 2
                        for f in range(NFT):
                            for di2 in range(2):
                                di = dh * 2 + di2
                                p.op("tensor",
                                     lambda e, wp2=wp2, f=f, di=di, q2=q2,
                                     ybase=ybase, di2=di2:
                                     e.matmul(y_ps[ybase + di2][:, :],
                                              w2_bf[wp2][:, f * 512 + di * 128:
                                                         f * 512 +
                                                         (di + 1) * 128],
                                              h_all[:, f * 1024 + q2 * 512:
                                                    f * 1024 + (q2 + 1) * 512],
                                              start=(f == 0),
                                              stop=(f == NFT - 1)),
                                     waits=([(sC["w2c"], w2c_counts[wi2]),
                                             sC["gate"] if routed
                                             else sC["relu"],
                                             (sC["accadd"],
                                              yuse.get(yset, 0))]
                                            if f == 0 else ()),
                                     inc=(sC["pe"] if (f == NFT - 1 and
                                                       di2 == 1) else None))
                        for di2 in range(2):
                            di = dh * 2 + di2
                            dcol = (dg * 4 + di) * 1024 + q2 * 512
                            if e_i == 0:
                                p.op("vector", lambda e, di2=di2, dcol=dcol,
                                     ybase=ybase:
                                     e.tensor_copy(acc[:, dcol:dcol + 512],
                                                   y_ps[ybase + di2][:, :]),
                                     waits=[sC["pe"]], inc=sC["accadd"])
                            else:
                                p.op("vector", lambda e, di2=di2, dcol=dcol,
                                     ybase=ybase:
                                     e.tensor_tensor(acc[:, dcol:dcol + 512],
                                                     acc[:, dcol:dcol + 512],
                                                     y_ps[ybase + di2][:, :],
                                                     op=ALU.add),
                                     waits=[sC["pe"]], inc=sC["accadd"])
                        yuse[yset] = sC["accadd"].n
        for dt in range(NDT):
            for q2 in range(2):
                p.op("tensor", lambda e, dt=dt, q2=q2:
                     e.matmul(y_ps[0][:, :], b2r_bf[:, dt * 128:(dt + 1) * 128],
                              gT_bf[:, half * 1024 + q2 * 512:
                                    half * 1024 + (q2 + 1) * 512],
                              start=True, stop=False),
                     waits=[sC["accadd"]], inc=None)
                p.op("tensor", lambda e, dt=dt, q2=q2:
                     e.matmul(y_ps[0][:, :], b2s_bf[:, dt * 128:(dt + 1) * 128],
                              onesK2[:, :], start=False, stop=True),
                     inc=sC["pe"])
                p.op("vector", lambda e, dt=dt, q2=q2:
                     e.tensor_tensor(acc[:, dt * 1024 + q2 * 512:
                                         dt * 1024 + (q2 + 1) * 512],
                                     acc[:, dt * 1024 + q2 * 512:
                                         dt * 1024 + (q2 + 1) * 512],
                                     y_ps[0][:, :], op=ALU.add),
                     waits=[sC["pe"]], inc=sC["accadd"])
        p.fence(f"fC{half}", [sC[k] for k in sC])
        scW.close()

        scF = p.scope()

        def sbF(name, shape, dtype):
            return scF.enter_context(nc.sbuf_tensor(name + str(half),
                                                    shape, dtype))

        f_tm = [sbF(f"ftm{i}", [128, D], F32) for i in range(2)]
        ux_t = [sbF(f"uxt{i}", [128, D], F32) for i in range(2)]
        xx_t = [sbF(f"xxt{i}", [128, D], F32) for i in range(2)]
        ln2 = sbF("ln2", [128, D], F32)
        y_t = sbF("y_t", [128, D], F32)
        ftps = [scF.enter_context(nc.psum_tensor(f"ftps{i}{half}",
                                                 [128, 128], F32))
                for i in range(2)]
        sF = sems_F

        ftps_use, ftm_use, ln2_use, yt_use = {}, {}, [0], [0]
        for tt in range(8):
            fp = tt % 2
            p.dma("gpsimd", xx_t[fp][:, :],
                  x1_scr[half * 1024 + tt * 128: half * 1024 + (tt + 1) * 128,
                         :], sF["xld"], waits=[(sF["dve"],
                                                ftm_use.get(fp, 0))])
            p.op("vector", lambda e: e.nop(), waits=[sF["xld"]])
            p.op("scalar", lambda e: e.nop(), waits=[sF["xld"]])
            emit_ln(p, xx_t[fp][:, :], ux_t[fp][:, :], stat[fp][:, :],
                    sq_scr[:, :], sF["dve"], sF["act"], eps_col[:, :])
            for dt in range(NDT):
                tp = (tt * NDT + dt) % 2
                p.op("tensor", lambda e, tp=tp, dt=dt, tt=tt:
                     e.transpose(ftps[tp][:, :],
                                 acc[:, dt * 1024 + tt * 128:
                                     dt * 1024 + (tt + 1) * 128],
                                 idf[:, :]),
                     waits=[y_st[half], (sF["trev"], ftps_use.get(tp, 0))],
                     inc=sF["tr"])
                p.op("scalar", lambda e, tp=tp, dt=dt, fp=fp:
                     e.activation(f_tm[fp][:, dt * 128:(dt + 1) * 128],
                                  ftps[tp][:, :], AF.Copy),
                     waits=[sF["tr"], (sF["dve"], ftm_use.get(fp, 0))],
                     inc=sF["trev"])
                ftps_use[tp] = sF["trev"].n
            p.op("vector", lambda e, fp=fp:
                 e.tensor_tensor(f_tm[fp][:, :], f_tm[fp][:, :],
                                 ux_t[fp][:, :], op=ALU.add),
                 waits=[sF["trev"], sF["dve"]], inc=sF["dve"])
            p.op("scalar", lambda e: e.nop(), waits=[sF["dve"]])
            emit_ln(p, f_tm[fp][:, :], ln2[:, :], stat[2 + fp][:, :],
                    sq_scr[:, :], sF["dve"], sF["act"], eps_col[:, :])
            ftm_use[fp] = sF["dve"].n
            p.op("vector", lambda e, fp=fp:
                 e.tensor_tensor(y_t[:, :], xx_t[fp][:, :], ln2[:, :],
                                 op=ALU.add),
                 waits=[sF["dve"], sF["xld"], y_st[half]], inc=sF["dve"])
            p.dma("scalar",
                  y[half * 1024 + tt * 128: half * 1024 + (tt + 1) * 128, :],
                  y_t[:, :], y_st[half], waits=[sF["dve"]])
        p.fence(f"fF{half}", list(sF.values()) + [y_st[half]])
        scF.close()

    p.op("gpsimd", lambda e: e.nop(), waits=[y_st[0], y_st[1]])


def build_l3():
    nc = bass.Bass("TRN2", target_bir_lowering=False)
    x = nc.dram_tensor("x", [TPC, D], F32, kind="ExternalInput").ap()
    oT = nc.dram_tensor("oT", [D, TPC], BF16, kind="ExternalInput").ap()
    woutr = nc.dram_tensor("woutr", [NDT, 128, NDT, 128], F32,
                           kind="ExternalInput").ap()
    sw1 = nc.dram_tensor("sw1", [2, NFT, 128, NDT, 128], F32,
                         kind="ExternalInput").ap()
    sb1 = nc.dram_tensor("sb1", [2, DFF], F32, kind="ExternalInput").ap()
    sw2 = nc.dram_tensor("sw2", [2, 4, 128, NFT, 512], F32,
                         kind="ExternalInput").ap()
    sb2 = nc.dram_tensor("sb2", [2, D], F32, kind="ExternalInput").ap()
    rw1 = nc.dram_tensor("rw1", [16, NFT, 128, NDT, 128], F32,
                         kind="ExternalInput").ap()
    rb1 = nc.dram_tensor("rb1", [16, DFF], F32, kind="ExternalInput").ap()
    rw2 = nc.dram_tensor("rw2", [16, 4, 128, NFT, 512], F32,
                         kind="ExternalInput").ap()
    rb2 = nc.dram_tensor("rb2", [16, D], F32, kind="ExternalInput").ap()
    cent = nc.dram_tensor("cent", [16, D], F32, kind="ExternalInput").ap()
    cbias = nc.dram_tensor("cbias", [1, 16], F32, kind="ExternalInput").ap()
    y = nc.dram_tensor("y", [TPC, D], F32, kind="ExternalOutput").ap()
    x1_scr = nc.dram_tensor("x1_scr", [TPC, D], F32, kind="Internal").ap()
    uT_scr = nc.dram_tensor("uT_scr", [D, TPC], BF16, kind="Internal").ap()
    ident_bf = nc.inline_tensor(np.eye(128, dtype=BF16NP), name="id_bf")
    ident_f = nc.inline_tensor(np.eye(128, dtype=np.float32), name="id_f")
    ones_r = nc.inline_tensor(np.ones((1, 128), np.float32), name="ones_r")
    ones_rb = nc.inline_tensor(np.ones((1, 128), dtype=BF16NP),
                               name="ones_rb")
    p = Prog(nc)
    emit_l3_body(p, nc, x, oT, woutr, sw1, sb1, sw2, sb2, rw1, rb1, rw2,
                 rb2, cent, cbias, y, x1_scr, uT_scr, ident_bf, ident_f,
                 ones_r, ones_rb)
    p.emit()
    p.close()
    return nc


def build_fused():
    nc = bass.Bass("TRN2", target_bir_lowering=False)
    xb = nc.dram_tensor("xb", [S, D], F32, kind="ExternalInput").ap()
    x = nc.dram_tensor("x", [TPC, D], F32, kind="ExternalInput").ap()
    wqkvr = nc.dram_tensor("wqkvr", [2 * NDT, 128, NDT, 128], F32,
                           kind="ExternalInput").ap()
    wvr = nc.dram_tensor("wvr", [128, NDT, D], F32,
                         kind="ExternalInput").ap()
    woutr = nc.dram_tensor("woutr", [NDT, 128, NDT, 128], F32,
                           kind="ExternalInput").ap()
    sw1 = nc.dram_tensor("sw1", [2, NFT, 128, NDT, 128], F32,
                         kind="ExternalInput").ap()
    sb1 = nc.dram_tensor("sb1", [2, DFF], F32, kind="ExternalInput").ap()
    sw2 = nc.dram_tensor("sw2", [2, 4, 128, NFT, 512], F32,
                         kind="ExternalInput").ap()
    sb2 = nc.dram_tensor("sb2", [2, D], F32, kind="ExternalInput").ap()
    rw1 = nc.dram_tensor("rw1", [16, NFT, 128, NDT, 128], F32,
                         kind="ExternalInput").ap()
    rb1 = nc.dram_tensor("rb1", [16, DFF], F32, kind="ExternalInput").ap()
    rw2 = nc.dram_tensor("rw2", [16, 4, 128, NFT, 512], F32,
                         kind="ExternalInput").ap()
    rb2 = nc.dram_tensor("rb2", [16, D], F32, kind="ExternalInput").ap()
    cent = nc.dram_tensor("cent", [16, D], F32, kind="ExternalInput").ap()
    cbias = nc.dram_tensor("cbias", [1, 16], F32, kind="ExternalInput").ap()
    y = nc.dram_tensor("y", [TPC, D], F32, kind="ExternalOutput").ap()

    qT_scr = nc.dram_tensor("qT_scr", [D, TPC], BF16, kind="Internal").ap()
    wqkb_scr = nc.dram_tensor("wqkb_scr", [2 * NDT, 128, NDT * 128], BF16,
                              kind="Internal").ap()
    kT_scr = nc.dram_tensor("kT_scr", [D, S], BF16, kind="Internal").ap()
    v_scr = nc.dram_tensor("v_scr", [S, D], BF16, kind="Internal").ap()
    oT_scr = nc.dram_tensor("oT_scr", [D, TPC], BF16, kind="Internal").ap()
    x1_scr = nc.dram_tensor("x1_scr", [TPC, D], F32, kind="Internal").ap()
    uT_scr = nc.dram_tensor("uT_scr", [D, TPC], BF16, kind="Internal").ap()

    ident_bf = nc.inline_tensor(np.eye(128, dtype=BF16NP), name="id_bf")
    ident_f = nc.inline_tensor(np.eye(128, dtype=np.float32), name="id_f")
    ones_r = nc.inline_tensor(np.ones((1, 128), np.float32), name="ones_r")
    ones_rb = nc.inline_tensor(np.ones((1, 128), dtype=BF16NP),
                               name="ones_rb")
    masks = nc.inline_tensor(causal_mask_np(), name="cmasks")

    p = Prog(nc)

    sc0 = p.scope()

    def a0(name, shape, dtype):
        return sc0.enter_context(nc.sbuf_tensor(name, shape, dtype))

    def ps0(name, shape, dtype=F32):
        return sc0.enter_context(nc.psum_tensor(name, shape, dtype))

    idb0 = a0("idb0", [128, 128], BF16)
    s_id0 = p.sem("q_id0")
    p.dma("gpsimd", idb0[:, :], ident_bf.ap(), s_id0)
    p.op("tensor", lambda e: e.nop(), waits=[(s_id0, 16)])
    c = qkv_ctx(p, a0, ps0, idb=idb0)
    emit_wv_cast(p, c, wvr)
    s_wpc = p.sem("q_wpc")
    for ct in range(2 * NDT):
        par = ct % 2
        p.dma("gpsimd", fm3(c["wst"][par][:, :], 128), wqkvr[ct],
              c["s_wld%d" % par], waits=[c["s_wqk%d" % par]])
        p.op("vector", lambda e, par=par:
             e.tensor_copy(c["wqk_bf"][par][:, :], c["wst"][par][:, :]),
             waits=[c["s_wld%d" % par], (s_wpc, 0 if ct < 2 else
                                         (ct - 2) * 16 + 16)],
             inc=c["s_wqk%d" % par])
        p.dma("scalar", wqkb_scr[ct], c["wqk_bf"][par][:, :], s_wpc,
              waits=[c["s_wqk%d" % par]])
    c["wquse"][0] = c["s_pe"].n
    c["wquse"][1] = c["s_pe"].n
    p.op("gpsimd", lambda e: e.nop(), waits=[(s_wpc, 2 * NDT * 16)])
    for sb_ in range(2):
        xb_ = sb_ % 2
        emit_xnT(p, c, x[sb_ * 1024:(sb_ + 1) * 1024, :], 1024, xb_)
        emit_qk(p, c, wqkb_scr, list(range(NDT)), qT_scr, 0, sb_ * 1024,
                1024, QSCALE, xb_, bf16_src=True)
    for tb in range(8):
        xb_ = tb % 2
        emit_xnT(p, c, xb[tb * 1024:(tb + 1) * 1024, :], 1024, xb_)
        emit_qk(p, c, wqkb_scr, list(range(NDT, 2 * NDT)), kT_scr, D,
                tb * 1024, 1024, 1.0, xb_, bf16_src=True)
        emit_v(p, c, v_scr, tb * 1024, 1024, xb_)
    f0 = p.fence("f_p0", [c["s_st0"], c["s_st1"], c["s_pe"], c["s_dve"],
                          c["s_act"], c["s_tr"], c["s_wv"]])
    sc0.close()

    sc1 = p.scope()

    def a1(name, shape, dtype):
        return sc1.enter_context(nc.sbuf_tensor(name, shape, dtype))

    common = {
        "q": [a1(f"aqb{i}", [128, 512], BF16) for i in range(3)],
        "k": [a1(f"akb{i}", [128, 512], BF16) for i in range(4)],
        "v": [a1(f"avb{i}", [128, 4 * 128], BF16) for i in range(4)],
        "pT": [a1(f"apT{i}", [128, 512], BF16) for i in range(3)],
        "o_ev": [a1(f"aoe{i}", [128, 512], BF16) for i in range(2)],
        "l_row": a1("alrow", [1, 512], BF16),
        "l_rowf": a1("alrowf", [1, 512], F32),
        "lb_sb": a1("albsb", [128, 512], BF16),
        "onesc": a1("aonesc", [128, 128], BF16),
        "masks": a1("amasks", [128, 4 * 512], BF16),
        "st": [sc1.enter_context(nc.psum_tensor(f"ast{i}", [128, 512],
                                                F32)) for i in range(3)],
        "ot": [sc1.enter_context(nc.psum_tensor(f"aot{i}", [128, 512],
                                                F32)) for i in range(2)],
        "l": [sc1.enter_context(nc.psum_tensor(f"al{i}", [1, 512],
                                               F32)) for i in range(2)],
        "lb": sc1.enter_context(nc.psum_tensor("alb", [128, 512], F32)),
    }
    s_mset = p.sem("a_mset")
    p.dma("gpsimd", common["masks"][:, :], masks.ap(), s_mset)
    p.op("vector", lambda e:
         e.memset(common["onesc"][:, :], 1.0), inc=s_mset, k=1)

    p1done = p.sem("p1done")
    sems = {k: p.sem("ap_" + k) for k in
            ("pe", "exp", "msk", "dve", "oev", "act", "ost")}
    sems["q"] = [p.sem(f"ap_q{i}") for i in range(3)]
    sems["k"] = [p.sem(f"ap_k{i}") for i in range(4)]
    sems["v"] = [p.sem(f"ap_v{i}") for i in range(4)]
    path_sems = ([sems[k] for k in
                  ("pe", "exp", "msk", "dve", "oev", "act", "ost")]
                 + sems["q"] + sems["k"] + sems["v"])
    snap = {id(s): s.n for s in path_sems}
    for j in range(4):
        p.set_path(j)
        for s_ in path_sems:
            s_.n = snap[id(s_)]
        bufs = dict(common)
        p.op("tensor", lambda e: e.nop(), waits=[(s_mset, s_mset.n)])
        for h in range(H):
            for ci_ch, ch in enumerate((j, 7 - j)):
                emit_attn_pair(
                    p, bufs, sems,
                    qT_scr[h * DH:(h + 1) * DH, :],
                    kT_scr[h * DH:(h + 1) * DH, :],
                    v_scr[:, h * DH:(h + 1) * DH],
                    oT_scr[h * DH:(h + 1) * DH,
                           ci_ch * 1024:(ci_ch + 1) * 1024],
                    chunks=[2 * ch, 2 * ch + 1],
                    qcols=[2 * ci_ch, 2 * ci_ch + 1])
        p.op("sync", lambda e: e.nop(), waits=[sems["ost"]])
        p.op("gpsimd", lambda e: e.nop(),
             waits=[(sems["k"][i], sems["k"][i].n) for i in range(4)] +
                   [(sems["q"][i], sems["q"][i].n) for i in range(3)])
        p.op("scalar", lambda e: e.nop(),
             waits=[(sems["v"][i], sems["v"][i].n) for i in range(4)])
        for eng in ENGS:
            self_inc = (lambda e: e.sem_inc(p1done.h, 1))
            p.op(eng, self_inc)
    p.set_path(None)
    p1done.n = 5
    for eng in ENGS:
        p.op(eng, lambda e: e.nop(), waits=[(p1done, 5)])
    sc1.close()

    emit_l3_body(p, nc, x, oT_scr, woutr, sw1, sb1, sw2, sb2, rw1, rb1,
                 rw2, rb2, cent, cbias, y, x1_scr, uT_scr, ident_bf,
                 ident_f, ones_r, ones_rb)
    p.emit()
    p.close()
    return nc


def _chunk_rows(j):
    a = np.arange(j * 1024, (j + 1) * 1024)
    b_ = np.arange((7 - j) * 1024, (8 - j) * 1024)
    return np.concatenate([a, b_])


_NC_CACHE = {}


def _get_nc(name):
    if name not in _NC_CACHE:
        _NC_CACHE[name] = {"l1": build_l1, "l2": build_l2,
                           "l3": build_l3, "fused": build_fused}[name]()
    return _NC_CACHE[name]


def _relay_w1(w1):
    e = w1.shape[0]
    return np.ascontiguousarray(
        w1.reshape(e, NDT, 128, NFT, 128).transpose(0, 3, 2, 1, 4))


def _relay_w2(w2):
    e = w2.shape[0]
    return np.ascontiguousarray(
        w2.reshape(e, NFT, 128, 4, 512).transpose(0, 3, 2, 1, 4))


def _relay_sq(w):
    c = w.shape[1]
    return np.ascontiguousarray(
        w.reshape(NDT, 128, c // 128, 128).transpose(2, 1, 0, 3))


USE_FUSED = True


def kernel(L0_attn_x, L0_attn_w_qkv, L0_attn_w_out,
           L0_ffn_shared_w1, L0_ffn_shared_b1, L0_ffn_shared_w2,
           L0_ffn_shared_b2, L0_ffn_routed_w1, L0_ffn_routed_b1,
           L0_ffn_routed_w2, L0_ffn_routed_b2, L0_ffn_centroids,
           L0_ffn_bias):
    x = np.ascontiguousarray(np.asarray(L0_attn_x, dtype=np.float32))
    wqkv = np.ascontiguousarray(np.asarray(L0_attn_w_qkv, dtype=np.float32))

    rows = [_chunk_rows(c % 4) for c in range(NCORES)]

    if USE_FUSED:
        wqkvr = _relay_sq(wqkv[:, :2 * D])
        wvr = np.ascontiguousarray(
            wqkv[:, 2 * D:].reshape(NDT, 128, D).transpose(1, 0, 2))
        woutr = _relay_sq(np.asarray(L0_attn_w_out, np.float32))
        sw1r = _relay_w1(np.asarray(L0_ffn_shared_w1, np.float32))
        sw2r = _relay_w2(np.asarray(L0_ffn_shared_w2, np.float32))
        rw1r = _relay_w1(np.asarray(L0_ffn_routed_w1, np.float32))
        rw2r = _relay_w2(np.asarray(L0_ffn_routed_w2, np.float32))
        ins = []
        for c in range(NCORES):
            b_ = c // 4
            ins.append({
                "xb": x[b_],
                "x": np.ascontiguousarray(x[b_][rows[c]]),
                "wqkvr": wqkvr, "wvr": wvr, "woutr": woutr,
                "sw1": sw1r, "sw2": sw2r, "rw1": rw1r, "rw2": rw2r,
                "sb1": np.ascontiguousarray(
                    np.asarray(L0_ffn_shared_b1, np.float32)),
                "sb2": np.ascontiguousarray(
                    np.asarray(L0_ffn_shared_b2, np.float32)),
                "rb1": np.ascontiguousarray(
                    np.asarray(L0_ffn_routed_b1, np.float32)),
                "rb2": np.ascontiguousarray(
                    np.asarray(L0_ffn_routed_b2, np.float32)),
                "cent": np.ascontiguousarray(
                    np.asarray(L0_ffn_centroids, np.float32)),
                "cbias": np.ascontiguousarray(
                    np.asarray(L0_ffn_bias, np.float32).reshape(1, 16)),
            })
        res = run_bass_kernel_spmd(_get_nc("fused"), ins,
                                   core_ids=list(range(NCORES)))
        out = np.empty((B, S, D), dtype=np.float32)
        for c in range(NCORES):
            out[c // 4][rows[c]] = res.results[c]["y"]
        return out

    wqkvr = _relay_sq(wqkv[:, :2 * D])
    wvr = np.ascontiguousarray(
        wqkv[:, 2 * D:].reshape(NDT, 128, D).transpose(1, 0, 2))
    in1 = [{"x": np.ascontiguousarray(x[c // 4][rows[c]]),
            "wqkvr": wqkvr, "wvr": wvr}
           for c in range(NCORES)]
    r1 = run_bass_kernel_spmd(_get_nc("l1"), in1, core_ids=list(range(NCORES)))
    qTg = np.empty((B, D, S), dtype=BF16NP)
    kTg = np.empty((B, D, S), dtype=BF16NP)
    vg = np.empty((B, S, D), dtype=BF16NP)
    for c in range(NCORES):
        b_ = c // 4
        qTg[b_][:, rows[c]] = r1.results[c]["qT"]
        kTg[b_][:, rows[c]] = r1.results[c]["kT"]
        vg[b_][rows[c]] = r1.results[c]["v"]

    in2 = []
    for c in range(NCORES):
        m = {}
        for i in range(4):
            pair = 4 * c + i
            b_, h_ = pair // H, pair % H
            m[f"q{i}"] = np.ascontiguousarray(qTg[b_, h_ * DH:(h_ + 1) * DH])
            m[f"k{i}"] = np.ascontiguousarray(kTg[b_, h_ * DH:(h_ + 1) * DH])
            m[f"v{i}"] = np.ascontiguousarray(vg[b_, :, h_ * DH:(h_ + 1) * DH])
        in2.append(m)
    r2 = run_bass_kernel_spmd(_get_nc("l2"), in2, core_ids=list(range(NCORES)))
    oTg = np.empty((B, D, S), dtype=BF16NP)
    for c in range(NCORES):
        for i in range(4):
            pair = 4 * c + i
            b_, h_ = pair // H, pair % H
            oTg[b_, h_ * DH:(h_ + 1) * DH] = r2.results[c][f"o{i}"]

    woutr = _relay_sq(np.asarray(L0_attn_w_out, np.float32))
    sw1r = _relay_w1(np.asarray(L0_ffn_shared_w1, np.float32))
    sw2r = _relay_w2(np.asarray(L0_ffn_shared_w2, np.float32))
    rw1r = _relay_w1(np.asarray(L0_ffn_routed_w1, np.float32))
    rw2r = _relay_w2(np.asarray(L0_ffn_routed_w2, np.float32))
    in3 = []
    for c in range(NCORES):
        b_ = c // 4
        in3.append({
            "x": np.ascontiguousarray(x[b_][rows[c]]),
            "oT": np.ascontiguousarray(oTg[b_][:, rows[c]]),
            "woutr": woutr,
            "sw1": sw1r,
            "sb1": np.ascontiguousarray(np.asarray(L0_ffn_shared_b1,
                                                   np.float32)),
            "sw2": sw2r,
            "sb2": np.ascontiguousarray(np.asarray(L0_ffn_shared_b2,
                                                   np.float32)),
            "rw1": rw1r,
            "rb1": np.ascontiguousarray(np.asarray(L0_ffn_routed_b1,
                                                   np.float32)),
            "rw2": rw2r,
            "rb2": np.ascontiguousarray(np.asarray(L0_ffn_routed_b2,
                                                   np.float32)),
            "cent": np.ascontiguousarray(np.asarray(L0_ffn_centroids,
                                                    np.float32)),
            "cbias": np.ascontiguousarray(
                np.asarray(L0_ffn_bias, np.float32).reshape(1, 16)),
        })
    r3 = run_bass_kernel_spmd(_get_nc("l3"), in3, core_ids=list(range(NCORES)))

    out = np.empty((B, S, D), dtype=np.float32)
    for c in range(NCORES):
        out[c // 4][rows[c]] = r3.results[c]["y"]
    return out


# version 11 (final)
# speedup vs baseline: 1.0387x; 1.0387x over previous
import math
from contextlib import ExitStack

import numpy as np
import ml_dtypes

import concourse.bass as bass
import concourse.mybir as mybir
from concourse.bass_utils import run_bass_kernel_spmd

F32 = mybir.dt.float32
BF16 = mybir.dt.bfloat16
AF = mybir.ActivationFunctionType
ALU = mybir.AluOpType
AX = mybir.AxisListType
BF16NP = ml_dtypes.bfloat16

B, S, D = 2, 8192, 2048
H, DH = 16, 128
DFF = 1408
NDT = D // 128
NFT = DFF // 128
NCORES = 8
TPC = 2048
EPS = 1e-5
QSCALE = 1.0 / math.sqrt(DH)
NE = 18

ENGS = ("sync", "scalar", "vector", "tensor", "gpsimd")


class Sem:
    __slots__ = ("h", "n", "name")

    def __init__(self, h, name):
        self.h, self.n, self.name = h, 0, name


class Op:
    __slots__ = ("eng", "fn", "waits", "incs", "path")

    def __init__(self, eng, fn, waits, incs, path=None):
        self.eng, self.fn, self.waits, self.incs = eng, fn, waits, incs
        self.path = path


class Prog:

    def __init__(self, nc):
        self.nc = nc
        self.ops = []
        self._es = ExitStack()
        self.cur_path = None
        self.n_paths = 0

    def sem(self, name):
        return Sem(self.nc.alloc_semaphore(name=name), name)

    def sb(self, name, shape, dtype):
        return self._es.enter_context(self.nc.sbuf_tensor(name, shape, dtype))

    def ps(self, name, shape, dtype=F32):
        return self._es.enter_context(self.nc.psum_tensor(name, shape, dtype))

    def scope(self):
        return self._es.enter_context(ExitStack())

    def op(self, eng, fn, waits=(), inc=None, k=1):
        ws = [(w, w.n) if isinstance(w, Sem) else w for w in waits]
        incs = []
        if inc is not None:
            incs = [(inc, k)]
            inc.n += k
        self.ops.append(Op(eng, fn, ws, incs, self.cur_path))

    def set_path(self, j):
        self.cur_path = j
        if j is not None:
            self.n_paths = max(self.n_paths, j + 1)

    def dma(self, q, out, in_, sem, waits=(), transpose=False):
        self.op(q, lambda e, o=out, i=in_, t=transpose:
                e.dma_start(out=o, in_=i, transpose=t), waits, inc=sem, k=16)

    def fence(self, name, sems):
        f = self.sem(name)
        ws = [(s, s.n) for s in sems]
        for eng in ENGS:
            self.op(eng, lambda e, s=f: e.sem_inc(s.h, 1), waits=list(ws))
            f.n += 1
        for eng in ENGS:
            self.op(eng, lambda e: e.nop(), waits=[(f, f.n)])
        return f

    def emit(self):
        nc = self.nc
        prog = self

        with nc.Block() as block:
            for engname in ENGS:
                def make(engname=engname):
                    def run(eng):
                        jreg = None
                        if prog.n_paths:
                            pid = eng.partition_id()
                            jreg = eng.alloc_register("jreg_" + engname)
                            eng.reg_alu(jreg, pid, 4, ALU.mod)
                        seen = {}
                        seen_common = {}
                        cur_path = None
                        guard = None
                        for o in prog.ops:
                            if o.eng != engname:
                                continue
                            if o.path != cur_path:
                                if guard is not None:
                                    guard.__exit__(None, None, None)
                                    guard = None
                                if o.path is not None:
                                    guard = eng.If_eq(jreg, o.path)
                                    guard.__enter__()
                                    seen = dict(seen_common)
                                else:
                                    seen = seen_common
                                cur_path = o.path
                            for (s, v) in o.waits:
                                if v > 0 and seen.get(id(s), 0) < v:
                                    eng.wait_ge(s.h, v)
                                    seen[id(s)] = v
                            inst = o.fn(eng)
                            for (s, k) in o.incs:
                                inst.then_inc(s.h, k)
                        if guard is not None:
                            guard.__exit__(None, None, None)
                        if prog.n_paths:
                            eng.end_ifs()
                    return run
                getattr(block, engname)(make())

    def close(self):
        self._es.close()


def wslice_fm(w_ap, c0, cw):
    return w_ap[:, c0:c0 + cw].rearrange("(dt p) c -> p dt c", p=128)


def fm3(sb_ap, cw):
    return sb_ap.rearrange("p (dt c) -> p dt c", c=cw)


def emit_ln(p, x_ap, out_ap, stat, sq_scr, s_dve, s_act, eps_col, out2_ap=None):
    p.op("vector", lambda e: e.reduce_sum(stat[:, 0:1], x_ap, axis=AX.X),
         inc=s_dve)
    p.op("scalar", lambda e: e.activation(sq_scr, x_ap, AF.Square,
                                          accum_out=stat[:, 1:2]),
         waits=[s_dve], inc=s_act)
    p.op("vector", lambda e: e.tensor_scalar(stat[:, 2:3], stat[:, 0:1],
                                             1.0 / D, None, op0=ALU.mult),
         waits=[s_dve], inc=s_dve)
    p.op("vector", lambda e: e.tensor_tensor(stat[:, 3:4], stat[:, 2:3],
                                             stat[:, 2:3], op=ALU.mult),
         waits=[s_dve], inc=s_dve)
    p.op("vector", lambda e: e.tensor_scalar(stat[:, 4:5], stat[:, 1:2],
                                             1.0 / D, stat[:, 3:4],
                                             op0=ALU.mult, op1=ALU.subtract),
         waits=[s_dve, s_act], inc=s_dve)
    p.op("scalar", lambda e: e.activation(stat[:, 5:6], stat[:, 4:5], AF.Sqrt,
                                          bias=eps_col), waits=[s_dve],
         inc=s_act)
    p.op("vector", lambda e: e.reciprocal(stat[:, 6:7], stat[:, 5:6]),
         waits=[s_act], inc=s_dve)
    p.op("vector", lambda e: e.tensor_scalar(out_ap, x_ap, stat[:, 2:3],
                                             stat[:, 6:7], op0=ALU.subtract,
                                             op1=ALU.mult),
         waits=[s_dve], inc=s_dve)
    if out2_ap is not None:
        p.op("vector", lambda e: e.tensor_scalar(out2_ap, x_ap, stat[:, 2:3],
                                                 stat[:, 6:7],
                                                 op0=ALU.subtract,
                                                 op1=ALU.mult),
             waits=[s_dve], inc=s_dve)


def qkv_ctx(p, alloc, psalloc=None, idb=None):
    c = {}
    c["x_sb"] = [alloc(f"qx{i}", [128, D], F32) for i in range(2)]
    c["xn_sb"] = [alloc(f"qxn{i}", [128, D], BF16) for i in range(2)]
    c["stat"] = [alloc(f"qst{i}", [128, 8], F32) for i in range(2)]
    c["sq"] = alloc("qsq", [128, D], F32)
    c["eps"] = alloc("qeps", [128, 1], F32)
    c["xnT"] = [alloc(f"qxnT{i}", [128, NDT * 1024], BF16)
                for i in range(2)]
    c["wv_bf"] = alloc("qwv", [128, NDT * D], BF16)
    c["wst"] = [alloc(f"qwst{i}", [128, D], F32) for i in range(2)]
    c["wqk_bf"] = [alloc(f"qwqk{i}", [128, NDT * 128], BF16)
                   for i in range(2)]
    c["ev"] = [alloc(f"qev{i}", [128, 512], BF16) for i in range(2)]
    if psalloc is None:
        psalloc = p.ps
    c["mm_ps"] = [psalloc(f"qmm{i}", [128, 512]) for i in range(2)]
    c["tr_ps"] = [psalloc(f"qtr{i}", [128, 128], BF16) for i in range(2)]
    for k in ("x0", "x1", "dve", "act", "tr", "wld0", "wld1", "wv",
              "wqk0", "wqk1", "pe", "ev0", "ev1", "st0", "st1"):
        c["s_" + k] = p.sem("q_" + k)
    c["wquse"] = {}
    c["xnT_use"] = {}
    c["idb"] = idb
    p.op("vector", lambda e: e.memset(c["eps"][:, :], EPS), inc=c["s_dve"])
    return c


def emit_wv_cast(p, c, wvr):
    for dt in range(NDT):
        par = dt % 2
        p.dma("gpsimd", c["wst"][par][:, :], wvr[:, dt, :],
              c["s_wld%d" % par], waits=[c["s_wv"]])
        p.op("vector", lambda e, par=par, dt=dt:
             e.tensor_copy(c["wv_bf"][:, dt * D:(dt + 1) * D],
                           c["wst"][par][:, :]),
             waits=[c["s_wld%d" % par]], inc=c["s_wv"])


def emit_xnT(p, c, x_ap, ntok, xb_):
    s_dve, s_act, s_tr = c["s_dve"], c["s_act"], c["s_tr"]
    xnT = c["xnT"][xb_]
    for r in range(ntok // 128):
        par = r % 2
        p.dma("gpsimd", c["x_sb"][par][:, :],
              x_ap[r * 128:(r + 1) * 128, :], c["s_x%d" % par],
              waits=[s_dve, s_act, s_tr])
        p.op("vector", lambda e: e.nop(),
             waits=[c["s_x%d" % par], s_tr,
                    (c["s_pe"], c["xnT_use"].get(xb_, 0))])
        p.op("scalar", lambda e: e.nop(), waits=[c["s_x%d" % par]])
        emit_ln(p, c["x_sb"][par][:, :], c["xn_sb"][par][:, :],
                c["stat"][par][:, :], c["sq"][:, :], s_dve, s_act,
                c["eps"][:, :])
        if c["idb"] is not None:
            for dt in range(NDT):
                tp = (r * NDT + dt) % 2
                p.op("tensor", lambda e, par=par, dt=dt, tp=tp:
                     e.transpose(c["tr_ps"][tp][:, :],
                                 c["xn_sb"][par][:, dt * 128:(dt + 1) * 128],
                                 c["idb"][:, :]),
                     waits=[s_dve, (s_tr, c.setdefault("_trx", {})
                            .get(tp, 0)),
                            (c["s_pe"], c["xnT_use"].get(xb_, 0))],
                     inc=c["s_pe"])
                p.op("scalar", lambda e, dt=dt, r=r, tp=tp, xnT=xnT:
                     e.activation(xnT[:, dt * 1024 + r * 128:
                                      dt * 1024 + (r + 1) * 128],
                                  c["tr_ps"][tp][:, :], AF.Copy),
                     waits=[c["s_pe"]], inc=s_tr)
                c["_trx"][tp] = s_tr.n
        else:
            for dt in range(NDT):
                p.dma("sync",
                      xnT[:, dt * 1024 + r * 128: dt * 1024 + (r + 1) * 128],
                      c["xn_sb"][par][:, dt * 128:(dt + 1) * 128], s_tr,
                      waits=[s_dve, (c["s_pe"], c["xnT_use"].get(xb_, 0))],
                      transpose=True)


def emit_qk(p, c, wqkvr, cts, dst, row_base, col0, ntok, scale, xb_,
            bf16_src=False):
    s_pe = c["s_pe"]
    xnT = c["xnT"][xb_]
    for ct in cts:
        par = ct % 2
        if bf16_src:
            p.dma("gpsimd", c["wqk_bf"][par][:, :], wqkvr[ct],
                  c["s_wqk%d" % par],
                  waits=[(s_pe, c["wquse"].get(par, 0))])
        else:
            p.dma("gpsimd", fm3(c["wst"][par][:, :], 128), wqkvr[ct],
                  c["s_wld%d" % par], waits=[c["s_wqk%d" % par]])
            p.op("vector", lambda e, par=par:
                 e.tensor_copy(c["wqk_bf"][par][:, :], c["wst"][par][:, :]),
                 waits=[c["s_wld%d" % par],
                        (s_pe, c["wquse"].get(par, 0))],
                 inc=c["s_wqk%d" % par])
        for ts in range(ntok // 512):
            pp = (ct * 4 + ts) % 2
            for dt in range(NDT):
                p.op("tensor", lambda e, par=par, pp=pp, ts=ts, dt=dt:
                     e.matmul(c["mm_ps"][pp][:, :],
                              c["wqk_bf"][par][:, dt * 128:(dt + 1) * 128],
                              xnT[:, dt * 1024 + ts * 512:
                                  dt * 1024 + (ts + 1) * 512],
                              start=(dt == 0), stop=(dt == NDT - 1)),
                     waits=([c["s_wqk%d" % par], c["s_tr"],
                             c["s_ev%d" % pp]] if dt == 0 else ()),
                     inc=(s_pe if dt == NDT - 1 else None))
            row0 = ct * 128 - row_base
            p.op("scalar", lambda e, pp=pp, sc=scale:
                 e.activation(c["ev"][pp][:, :], c["mm_ps"][pp][:, :],
                              AF.Copy, scale=sc),
                 waits=[s_pe, c["s_st%d" % pp]], inc=c["s_ev%d" % pp])
            p.dma("scalar",
                  dst[row0:row0 + 128, col0 + ts * 512:col0 + (ts + 1) * 512],
                  c["ev"][pp][:, :], c["s_st%d" % pp],
                  waits=[c["s_ev%d" % pp]])
        c["wquse"][par] = s_pe.n
    c["xnT_use"][xb_] = s_pe.n


def emit_v(p, c, v_dst, row0_, ntok, xb_):
    s_pe = c["s_pe"]
    xnT = c["xnT"][xb_]
    for tt in range(ntok // 128):
        for vc in range(4):
            pp = (tt * 4 + vc) % 2
            for dt in range(NDT):
                p.op("tensor", lambda e, pp=pp, tt=tt, vc=vc, dt=dt:
                     e.matmul(c["mm_ps"][pp][:, :],
                              xnT[:, dt * 1024 + tt * 128:
                                  dt * 1024 + (tt + 1) * 128],
                              c["wv_bf"][:, dt * D + vc * 512:
                                         dt * D + (vc + 1) * 512],
                              start=(dt == 0), stop=(dt == NDT - 1)),
                     waits=([c["s_wv"], c["s_ev%d" % pp]] if dt == 0
                            else ()),
                     inc=(s_pe if dt == NDT - 1 else None))
            p.op("scalar", lambda e, pp=pp:
                 e.activation(c["ev"][pp][:, :], c["mm_ps"][pp][:, :],
                              AF.Copy),
                 waits=[s_pe, c["s_st%d" % pp]], inc=c["s_ev%d" % pp])
            p.dma("scalar",
                  v_dst[row0_ + tt * 128:row0_ + (tt + 1) * 128,
                        vc * 512:(vc + 1) * 512],
                  c["ev"][pp][:, :], c["s_st%d" % pp],
                  waits=[c["s_ev%d" % pp]])
    c["xnT_use"][xb_] = s_pe.n


def build_l1():
    nc = bass.Bass("TRN2", target_bir_lowering=False)
    x = nc.dram_tensor("x", [TPC, D], F32, kind="ExternalInput").ap()
    wqkvr = nc.dram_tensor("wqkvr", [2 * NDT, 128, NDT, 128], F32,
                           kind="ExternalInput").ap()
    wvr = nc.dram_tensor("wvr", [128, NDT, D], F32,
                         kind="ExternalInput").ap()
    qT = nc.dram_tensor("qT", [D, TPC], BF16, kind="ExternalOutput").ap()
    kT = nc.dram_tensor("kT", [D, TPC], BF16, kind="ExternalOutput").ap()
    v = nc.dram_tensor("v", [TPC, D], BF16, kind="ExternalOutput").ap()

    p = Prog(nc)
    c = qkv_ctx(p, p.sb)
    emit_wv_cast(p, c, wvr)
    for sb_ in range(2):
        xb_ = sb_ % 2
        emit_xnT(p, c, x[sb_ * 1024:(sb_ + 1) * 1024, :], 1024, xb_)
        emit_qk(p, c, wqkvr, list(range(NDT)), qT, 0, sb_ * 1024, 1024,
                QSCALE, xb_)
        emit_qk(p, c, wqkvr, list(range(NDT, 2 * NDT)), kT, D, sb_ * 1024,
                1024, 1.0, xb_)
        emit_v(p, c, v, sb_ * 1024, 1024, xb_)
    p.op("gpsimd", lambda e: e.nop(), waits=[c["s_st0"], c["s_st1"]])
    p.emit()
    p.close()
    return nc


def causal_mask_np():
    kk = np.arange(128)[:, None]
    qq = np.arange(512)[None, :]
    ms = [(kk + d <= qq).astype(BF16NP) for d in (0, 128, 256, 384)]
    return np.concatenate(ms, axis=1)


def emit_attn_pair(p, bufs, sems, q_src, k_src, v_src, o_dst, chunks,
                   qcols=None):
    (q_sb, k_sb, v_sb, pT, o_ev, tr_sb, l_r, masks_sb, st_ps, oacc_ps) = (
        bufs["q"], bufs["k"], bufs["v"], bufs["pT"], bufs["o_ev"],
        bufs["tr"], bufs["l_r"], bufs["masks"], bufs["st"], bufs["oacc"])
    s_q, s_k, s_v, s_pe, s_exp, s_msk, s_dve, s_oev, s_tr, s_ost = (
        sems["q"], sems["k"], sems["v"], sems["pe"], sems["exp"],
        sems["msk"], sems["dve"], sems["oev"], sems["tr"], sems["ost"])

    if qcols is None:
        qcols = chunks
    nbuf = len(k_sb)
    st = bufs.setdefault("_state", {
        "kv_i": 0, "sp": 0, "kuse": {}, "quse": {}, "puse": {}, "suse": {},
        "oevq": {}, "dveq": {}})
    q_i = [0]

    def load_block(bi, kb):
        kp = bi % nbuf
        kw = [(s_pe, st["kuse"].get(kp, 0))]
        p.dma("gpsimd", k_sb[kp][:, :],
              k_src[:, kb * 512:(kb + 1) * 512], s_k[kp], waits=kw)
        for t4 in range(4):
            p.dma("scalar", v_sb[kp][:, t4 * 129:t4 * 129 + 128],
                  v_src[kb * 512 + t4 * 128: kb * 512 + (t4 + 1) * 128, :],
                  s_v[kp], waits=kw)

    def load_q(ci):
        if ci >= len(chunks):
            return
        qp = ci % 3
        p.dma("gpsimd", q_sb[qp][:, :],
              q_src[:, qcols[ci] * 512:(qcols[ci] + 1) * 512], s_q[qp],
              waits=[(s_pe, st["quse"].get(qp, 0))])

    load_q(0)
    load_q(1)
    q_i[0] = 2

    for ci, qs in enumerate(chunks):
        qp = ci % 3
        work = []
        for kb in range(qs + 1):
            for kt in range(4):
                delta = (kb * 512 + kt * 128) - qs * 512
                if delta < 512:
                    work.append((kb, kt, delta))
        starts = {}
        for wi, (kb, kt, _) in enumerate(work):
            if kb not in starts:
                starts[kb] = wi
        issue_at = {}
        for j, (kb, w0) in enumerate(sorted(starts.items())):
            issue_at.setdefault(max(0, w0 - 8), []).append(kb)
        bis = {}

        def s_tile(wi):
            kb, kt, delta = work[wi]
            for kb2 in issue_at.get(wi, ()):
                bi = st["kv_i"]
                st["kv_i"] += 1
                bis[kb2] = bi
                load_block(bi, kb2)
            kp = bis[kb] % nbuf
            sp = st["sp"] % 3
            st["sp"] += 1
            p.op("tensor", lambda e, sp=sp, kp=kp, qp=qp, kt=kt:
                 e.matmul(st_ps[sp][:, :],
                          k_sb[kp][:, kt * 128:(kt + 1) * 128],
                          q_sb[qp][:, :], start=True, stop=True),
                 waits=[s_k[kp], s_q[qp], (s_exp, st["suse"].get(sp, 0))],
                 inc=s_pe)
            ppe, pmsk = st["puse"].get(sp, (0, 0))
            p.op("scalar", lambda e, sp=sp:
                 e.activation(pT[sp][:, :], st_ps[sp][:, :], AF.Exp),
                 waits=[s_pe, (s_pe, ppe), (s_msk, pmsk)], inc=s_exp)
            st["suse"][sp] = s_exp.n
            if delta >= 0:
                mi = delta // 128
                p.op("vector", lambda e, sp=sp, mi=mi:
                     e.tensor_tensor(pT[sp][:, :], pT[sp][:, :],
                                     masks_sb[:, mi * 512:(mi + 1) * 512],
                                     op=ALU.mult),
                     waits=[s_exp], inc=s_msk)
            return (sp, s_exp.n, s_msk.n)

        def av_tile(wi, spt, first, last):
            sp, expn, mskn = spt
            kb, kt, delta = work[wi]
            kp = bis[kb] % nbuf
            pw = ([(s_msk, mskn)] if delta >= 0 else [(s_exp, expn)])
            for qt in range(4):
                fw = []
                if first:
                    fw = [(s_oev, st["oevq"].get(qt, 0)),
                          (s_dve, st["dveq"].get(qt, 0))]
                p.op("tensor", lambda e, sp=sp, kp=kp, kt=kt, qt=qt,
                     first=first, last=last:
                     e.matmul(oacc_ps[qt][:, :],
                              pT[sp][:, qt * 128:(qt + 1) * 128],
                              v_sb[kp][:, kt * 129:kt * 129 + 129],
                              start=first, stop=last),
                     waits=(pw + [s_v[kp]] + fw),
                     inc=(s_pe if (last and qt == 3) else None))
            st["puse"][sp] = (s_pe.n, s_msk.n)
            st["kuse"][kp] = s_pe.n

        n = len(work)
        pend = [s_tile(0)]
        if n > 1:
            pend.append(s_tile(1))
        for wi in range(2, n):
            pend.append(s_tile(wi))
            spt = pend.pop(0)
            av_tile(wi - 2, spt, first=(wi - 2 == 0), last=False)
            if wi == n // 2:
                load_q(q_i[0])
                q_i[0] += 1
        for k2, spt in enumerate(pend):
            wi2 = n - len(pend) + k2
            av_tile(wi2, spt, first=(wi2 == 0), last=(wi2 == n - 1))
        if n <= 2:
            load_q(q_i[0])
            q_i[0] += 1
        st["quse"][qp] = s_pe.n

        for qt in range(4):
            op = (ci * 4 + qt) % 2
            p.op("vector", lambda e, qt=qt:
                 e.reciprocal(l_r[:, qt:qt + 1], oacc_ps[qt][:, 128:129]),
                 waits=[s_pe], inc=s_dve)
            p.op("vector", lambda e, qt=qt, op=op:
                 e.tensor_scalar(o_ev[op][:, :], oacc_ps[qt][:, 0:128],
                                 l_r[:, qt:qt + 1], None, op0=ALU.mult),
                 waits=[s_dve, (s_tr, st.get("_tru_%d" % op, 0))],
                 inc=s_oev)
            st["oevq"][qt] = s_oev.n
            st["dveq"][qt] = s_dve.n
            p.dma("sync", tr_sb[op][:, :], o_ev[op][:, :], s_tr,
                  waits=[s_oev, (s_ost, st.get("_osu_%d" % op, 0))],
                  transpose=True)
            st["_tru_%d" % op] = s_tr.n
            p.dma("sync",
                  o_dst[:, ci * 512 + qt * 128: ci * 512 + (qt + 1) * 128],
                  tr_sb[op][:, :], s_ost, waits=[s_tr])
            st["_osu_%d" % op] = s_ost.n


def build_l2():
    nc = bass.Bass("TRN2", target_bir_lowering=False)
    qs_ = [nc.dram_tensor(f"q{i}", [128, S], BF16, kind="ExternalInput").ap()
           for i in range(4)]
    ks_ = [nc.dram_tensor(f"k{i}", [128, S], BF16, kind="ExternalInput").ap()
           for i in range(4)]
    vs_ = [nc.dram_tensor(f"v{i}", [S, 128], BF16, kind="ExternalInput").ap()
           for i in range(4)]
    os_ = [nc.dram_tensor(f"o{i}", [128, S], BF16, kind="ExternalOutput").ap()
           for i in range(4)]
    masks = nc.inline_tensor(causal_mask_np(), name="cmasks")

    p = Prog(nc)
    bufs = {
        "q": [p.sb(f"qb{i}", [128, 512], BF16) for i in range(3)],
        "k": [p.sb(f"kb{i}", [128, 512], BF16) for i in range(4)],
        "v": [p.sb(f"vb{i}", [128, 4 * 129], BF16) for i in range(4)],
        "pT": [p.sb(f"pT{i}", [128, 512], BF16) for i in range(3)],
        "o_ev": [p.sb(f"oe{i}", [128, 128], BF16) for i in range(2)],
        "tr": [p.sb(f"tr{i}", [128, 128], BF16) for i in range(2)],
        "l_r": p.sb("l_r", [128, 4], F32),
        "masks": p.sb("masks", [128, 4 * 512], BF16),
        "st": [p.ps(f"st{i}", [128, 512]) for i in range(3)],
        "oacc": [p.ps(f"oa{i}", [128, 129]) for i in range(4)],
    }
    sems = {k: p.sem("s_" + k) for k in
            ("pe", "exp", "msk", "dve", "oev", "tr", "ost", "ld")}
    sems["q"] = [p.sem(f"s_q{i}") for i in range(3)]
    sems["k"] = [p.sem(f"s_k{i}") for i in range(4)]
    sems["v"] = [p.sem(f"s_v{i}") for i in range(4)]

    p.dma("gpsimd", bufs["masks"][:, :], masks.ap(), sems["ld"])
    for i in range(4):
        for t4 in range(4):
            p.op("vector", lambda e, i=i, t4=t4:
                 e.memset(bufs["v"][i][:, t4 * 129 + 128:t4 * 129 + 129],
                          1.0), inc=sems["msk"])
    p.op("vector", lambda e: e.nop(), waits=[sems["ld"]])
    p.op("tensor", lambda e: e.nop(), waits=[sems["msk"], sems["ld"]])

    for pair in range(4):
        emit_attn_pair(p, bufs, sems, qs_[pair], ks_[pair], vs_[pair],
                       os_[pair], chunks=list(range(S // 512)))

    p.op("gpsimd", lambda e: e.nop(), waits=[sems["ost"]])
    p.emit()
    p.close()
    return nc


def emit_l3_body(p, nc, x, oT, woutr, sw1, sb1, sw2, sb2, rw1, rb1, rw2,
                 rb2, cent, cbias, y, x1_scr, uT_scr, ident_bf, ident_f,
                 ones_r, ones_rb):

    idb = p.sb("idb", [128, 128], BF16)
    idf = p.sb("idf", [128, 128], F32)
    ones_sb = p.sb("ones", [1, 128], F32)
    ones_bsb = p.sb("onesb", [1, 128], BF16)
    gT_bf = p.sb("gTb", [16, TPC], BF16)
    b1T = p.sb("b1T", [128, NFT * NE], F32)
    b2r_bf = p.sb("b2r", [16, D], BF16)
    b2s_bf = p.sb("b2s", [2, D], BF16)
    onesK2 = p.sb("onesK2", [2, 512], BF16)
    stat = [p.sb(f"st{i}", [128, 8], F32) for i in range(4)]
    sq_scr = p.sb("sq", [128, D], BF16)
    eps_col = p.sb("eps", [128, 1], F32)

    s = {k: p.sem("s_" + k) for k in
         ("ld", "pe", "act", "dve", "pool", "tr", "st", "prm")}

    centT = p.sb("centT", [128, NDT * 16], BF16)
    bias_pl = p.sb("biaspl", [128, 16], F32)
    scP = p.scope()
    prm = scP.enter_context(nc.sbuf_tensor("prm", [16, 2 * D + DFF], F32))
    prm2 = scP.enter_context(nc.sbuf_tensor("prm2", [2, D + DFF], F32))
    cent_bf = scP.enter_context(nc.sbuf_tensor("centb", [16, D], BF16))
    bias_row = scP.enter_context(nc.sbuf_tensor("biasrow", [1, 16], F32))
    tp_ps = scP.enter_context(nc.psum_tensor("tp_ps", [128, 128], F32))
    tpb_ps = scP.enter_context(nc.psum_tensor("tpb_ps", [128, 128], BF16))

    p.op("vector", lambda e: e.memset(eps_col[:, :], EPS), inc=s["dve"])
    p.dma("gpsimd", idb[:, :], ident_bf.ap(), s["ld"])
    p.dma("gpsimd", idf[:, :], ident_f.ap(), s["ld"])
    p.dma("gpsimd", ones_sb[:, :], ones_r.ap(), s["ld"])
    p.dma("gpsimd", ones_bsb[:, :], ones_rb.ap(), s["ld"])
    p.dma("gpsimd", prm[:, 0:D], cent, s["ld"])
    p.dma("gpsimd", prm[:, D:2 * D], rb2, s["ld"])
    p.dma("gpsimd", prm[:, 2 * D:2 * D + DFF], rb1, s["ld"])
    p.dma("gpsimd", prm2[:, 0:D], sb2, s["ld"])
    p.dma("gpsimd", prm2[:, D:D + DFF], sb1, s["ld"])
    p.dma("gpsimd", bias_row[:, :], cbias, s["ld"])
    p.op("vector", lambda e: e.tensor_copy(cent_bf[:, :], prm[:, 0:D]),
         waits=[s["ld"]], inc=s["dve"])
    p.op("vector", lambda e: e.tensor_copy(b2r_bf[:, :], prm[:, D:2 * D]),
         waits=[s["ld"]], inc=s["dve"])
    p.op("vector", lambda e: e.tensor_copy(b2s_bf[:, :], prm2[:, 0:D]),
         waits=[s["ld"]], inc=s["dve"])
    p.op("vector", lambda e: e.memset(onesK2[:, :], 1.0), inc=s["dve"])
    for dt in range(NDT):
        p.op("tensor", lambda e, dt=dt:
             e.transpose(tpb_ps[:, 0:16], cent_bf[:, dt * 128:(dt + 1) * 128],
                         idb[0:16, 0:16]),
             waits=[s["dve"], s["ld"], s["act"]], inc=s["pe"])
        p.op("scalar", lambda e, dt=dt:
             e.activation(centT[:, dt * 16:(dt + 1) * 16], tpb_ps[:, 0:16],
                          AF.Copy), waits=[s["pe"]], inc=s["act"])
    for f in range(NFT):
        p.op("tensor", lambda e, f=f:
             e.transpose(tp_ps[:, 0:16], prm[:, 2 * D + f * 128:
                                             2 * D + (f + 1) * 128],
                         idf[0:16, 0:16]),
             waits=[s["ld"], s["act"]], inc=s["pe"])
        p.op("scalar", lambda e, f=f:
             e.activation(b1T[:, f * NE:f * NE + 16], tp_ps[:, 0:16],
                          AF.Copy), waits=[s["pe"]], inc=s["act"])
        p.op("tensor", lambda e, f=f:
             e.transpose(tp_ps[:, 0:2], prm2[:, D + f * 128:
                                             D + (f + 1) * 128],
                         idf[0:2, 0:2]),
             waits=[s["ld"], s["act"]], inc=s["pe"])
        p.op("scalar", lambda e, f=f:
             e.activation(b1T[:, f * NE + 16:f * NE + 18], tp_ps[:, 0:2],
                          AF.Copy), waits=[s["pe"]], inc=s["act"])
    p.op("tensor", lambda e:
         e.matmul(tp_ps[:, 0:16], ones_sb[:, :], bias_row[:, :],
                  start=True, stop=True),
         waits=[s["ld"], s["act"]], inc=s["pe"])
    p.op("scalar", lambda e:
         e.activation(bias_pl[:, :], tp_ps[:, 0:16], AF.Copy),
         waits=[s["pe"]], inc=s["act"])
    p.fence("fP", [s["pe"], s["act"], s["dve"], s["ld"]])
    scP.close()

    scA = p.scope()

    def sbA(name, shape, dtype):
        return scA.enter_context(nc.sbuf_tensor(name, shape, dtype))

    oT_sb = [sbA(f"oTs{i}", [128, NDT * 512], BF16) for i in range(2)]
    wo_st = [sbA(f"wost{i}", [128, D], F32) for i in range(2)]
    wo_bf = [sbA(f"wob{i}", [128, D], BF16) for i in range(2)]
    aoev = [sbA(f"aoev{i}", [128, 512], F32) for i in range(2)]
    ao_tm = [sbA(f"aotm{i}", [128, D], F32) for i in range(8)]
    a_t = sbA("a_t", [128, D], F32)
    x_t = [sbA(f"x_t{i}", [128, D], F32) for i in range(2)]
    x1_t = sbA("x1_t", [128, D], F32)
    u_bf = [sbA(f"u_bf{i}", [128, D], BF16) for i in range(2)]
    uT_tile = [sbA(f"uTt{i}", [128, NDT * 128], BF16) for i in range(2)]
    aops = scA.enter_context(nc.psum_tensor("aops", [128, 512], F32))
    aops2 = scA.enter_context(nc.psum_tensor("aops2", [128, 512], F32))
    trps = [scA.enter_context(nc.psum_tensor(f"trps{i}", [128, 128], F32))
            for i in range(2)]
    sA = {k: p.sem("sA_" + k) for k in
          ("oT", "wld", "wc", "pe", "ev", "tr", "trev", "xld", "dve", "act",
           "ubf", "utr", "str", "ustr")}

    wouse, aoev_use, trps_use, aotm_use = {}, {}, {}, {}
    ustr_tile = {}
    ubuse, utile_use = {}, {}

    def a_mm_block(ts):

        otp = ts % 2
        p.dma("gpsimd", fm3(oT_sb[otp][:, :], 512),
              oT[:, ts * 512:(ts + 1) * 512]
              .rearrange("(dt p) t -> p dt t", p=128),
              sA["oT"], waits=[sA["pe"]])
        for dto in range(NDT):
            wp = dto % 2
            psA_ = aops if (dto % 2 == 0) else aops2
            p.dma("gpsimd", fm3(wo_st[wp][:, :], 128),
                  woutr[dto], sA["wld"], waits=[sA["wc"]])
            p.op("vector", lambda e, wp=wp:
                 e.tensor_copy(wo_bf[wp][:, :], wo_st[wp][:, :]),
                 waits=[sA["wld"], (sA["pe"], wouse.get(wp, 0))],
                 inc=sA["wc"])
            for dt in range(NDT):
                p.op("tensor", lambda e, psA_=psA_, wp=wp, otp=otp, dt=dt:
                     e.matmul(psA_[:, :], wo_bf[wp][:, dt * 128:(dt + 1) * 128],
                              oT_sb[otp][:, dt * 512:(dt + 1) * 512],
                              start=(dt == 0), stop=(dt == NDT - 1)),
                     waits=([sA["wc"], sA["oT"], sA["ev"]] if dt == 0 else ()),
                     inc=(sA["pe"] if dt == NDT - 1 else None))
            wouse[wp] = sA["pe"].n
            ep = dto % 2
            p.op("scalar", lambda e, psA_=psA_, ep=ep:
                 e.activation(aoev[ep][:, :], psA_[:, :], AF.Copy),
                 waits=[sA["pe"], (sA["tr"], aoev_use.get(ep, 0))],
                 inc=sA["ev"])
            for tt in range(4):
                tp = (dto * 4 + tt) % 2
                p.op("tensor", lambda e, ep=ep, tt=tt, tp=tp:
                     e.transpose(trps[tp][:, :],
                                 aoev[ep][:, tt * 128:(tt + 1) * 128],
                                 idf[:, :]),
                     waits=[sA["ev"], (sA["trev"], trps_use.get(tp, 0))],
                     inc=sA["tr"])
                p.op("scalar", lambda e, ts=ts, tt=tt, tp=tp, dto=dto:
                     e.activation(ao_tm[(ts % 2) * 4 + tt]
                                  [:, dto * 128:(dto + 1) * 128],
                                  trps[tp][:, :], AF.Copy),
                     waits=[sA["tr"],
                            (sA["dve"],
                             aotm_use.get((ts % 2) * 4 + tt, 0))],
                     inc=sA["trev"])
                trps_use[tp] = sA["trev"].n
            aoev_use[ep] = sA["tr"].n

    def a_ln_block(ts):
        for tt in range(4):
            r0 = ts * 512 + tt * 128
            xp = tt % 2
            p.dma("gpsimd", x_t[xp][:, :], x[r0:r0 + 128, :], sA["xld"],
                  waits=[sA["dve"]])
            p.op("vector", lambda e: e.nop(), waits=[sA["trev"], sA["xld"]])
            p.op("scalar", lambda e: e.nop(), waits=[sA["trev"]])
            emit_ln(p, ao_tm[(ts % 2) * 4 + tt][:, :], a_t[:, :], stat[tt % 2][:, :],
                    sq_scr[:, :], sA["dve"], sA["act"], eps_col[:, :])
            aotm_use[(ts % 2) * 4 + tt] = sA["dve"].n
            p.op("vector", lambda e, xp=xp:
                 e.tensor_tensor(x1_t[:, :], x_t[xp][:, :], a_t[:, :],
                                 op=ALU.add),
                 waits=[sA["dve"], sA["str"], sA["act"]], inc=sA["dve"])
            p.dma("scalar", x1_scr[r0:r0 + 128, :], x1_t[:, :], sA["str"],
                  waits=[sA["dve"]])
            p.op("scalar", lambda e: e.nop(), waits=[sA["dve"]])
            up = (ts * 4 + tt) % 2
            p.op("vector", lambda e: e.nop(),
                 waits=[(sA["utr"], ubuse.get(up, 0)), sA["str"]])
            emit_ln(p, x1_t[:, :], u_bf[up][:, :], stat[2 + tt % 2][:, :],
                    sq_scr[:, :], sA["dve"], sA["act"], eps_col[:, :])
            for dt in range(NDT):
                p.dma("sync",
                      uT_tile[up][:, dt * 128:(dt + 1) * 128],
                      u_bf[up][:, dt * 128:(dt + 1) * 128], sA["utr"],
                      waits=([sA["dve"],
                              (sA["ustr"], utile_use.get(up, 0))]
                             if dt == 0 else [sA["dve"]]), transpose=True)
            ubuse[up] = sA["utr"].n
            p.dma("scalar",
                  uT_scr[:, r0:r0 + 128]
                  .rearrange("(dt p) t -> p dt t", p=128),
                  fm3(uT_tile[up][:, :], 128), sA["ustr"],
                  waits=[sA["utr"]])
            ustr_tile[r0 // 128] = sA["ustr"].n
            utile_use[up] = sA["ustr"].n


    for ts in range(4):
        a_mm_block(ts)
        if ts >= 1:
            a_ln_block(ts - 1)
    a_ln_block(3)
    a_final = {k: (sA[k], sA[k].n) for k in
               ("pe", "ev", "tr", "trev", "dve", "act", "str", "ustr")}
    scA.close()

    scB = p.scope()

    def sbB(name, shape, dtype):
        return scB.enter_context(nc.sbuf_tensor(name, shape, dtype))

    uTt = [sbB(f"uTb{i}", [128, NDT * 128], BF16) for i in range(2)]
    s0 = sbB("s0", [128, 16], F32)
    sa = sbB("sa", [128, 16], F32)
    sb_ = sbB("sb", [128, 16], F32)
    t16 = sbB("t16", [128, 16], F32)
    g16 = sbB("g16", [128, 16], F32)
    gst = sbB("gst", [128, 8], F32)
    scps = scB.enter_context(nc.psum_tensor("scps", [128, 16], F32))
    gtps = scB.enter_context(nc.psum_tensor("gtps", [128, 128], F32))
    sB = {k: p.sem("sB_" + k) for k in ("pe", "act", "dve", "tr", "uld")}

    for tt in range(TPC // 128):
        up = tt % 2
        p.dma("gpsimd", fm3(uTt[up][:, :], 128),
              uT_scr[:, tt * 128:(tt + 1) * 128]
              .rearrange("(dt p) t -> p dt t", p=128),
              sB["uld"], waits=[sB["pe"],
                                (sA["ustr"], ustr_tile[tt])])
        for dt in range(NDT):
            p.op("tensor", lambda e, up=up, dt=dt:
                 e.matmul(scps[:, :],
                          uTt[up][:, dt * 128:(dt + 1) * 128],
                          centT[:, dt * 16:(dt + 1) * 16],
                          start=(dt == 0), stop=(dt == NDT - 1)),
                 waits=([sB["act"], sB["uld"]] + ([a_final["ev"],
                                                   a_final["trev"],
                                                   a_final["tr"]]
                                                  if tt == 0 else [])
                        if dt == 0 else ()),
                 inc=(sB["pe"] if dt == NDT - 1 else None))
        p.op("scalar", lambda e:
             e.activation(sa[:, :], scps[:, :], AF.Sigmoid),
             waits=[sB["pe"], sB["dve"]] + ([a_final["act"],
                                             a_final["dve"]]
                                            if tt == 0 else []),
             inc=sB["act"])
        p.op("vector", lambda e:
             e.tensor_tensor(s0[:, :], sa[:, :], bias_pl[:, :], op=ALU.add),
             waits=[sB["act"], sB["tr"]], inc=sB["dve"])
        cur = s0
        for r in range(4):
            p.op("vector", lambda e, cur=cur, r=r:
                 e.reduce_max(gst[:, r:r + 1], cur[:, :], axis=AX.X),
                 waits=[sB["dve"]], inc=sB["dve"])
            if r < 3:
                nxt = sb_ if cur is not sb_ else sa
                p.op("vector", lambda e, cur=cur, r=r:
                     e.tensor_scalar(t16[:, :], cur[:, :], gst[:, r:r + 1],
                                     None, op0=ALU.is_ge),
                     waits=[sB["dve"]], inc=sB["dve"])
                p.op("vector", lambda e, cur=cur, nxt=nxt:
                     e.scalar_tensor_tensor(nxt[:, :], t16[:, :], -1e30,
                                            cur[:, :], op0=ALU.mult,
                                            op1=ALU.add),
                     waits=[sB["dve"]], inc=sB["dve"])
                cur = nxt
        p.op("vector", lambda e:
             e.tensor_scalar(t16[:, :], s0[:, :], gst[:, 3:4], None,
                             op0=ALU.is_ge), waits=[sB["dve"]], inc=sB["dve"])
        p.op("vector", lambda e:
             e.tensor_tensor(g16[:, :], s0[:, :], t16[:, :], op=ALU.mult),
             waits=[sB["dve"]], inc=sB["dve"])
        p.op("vector", lambda e:
             e.reduce_sum(gst[:, 4:5], g16[:, :], axis=AX.X),
             waits=[sB["dve"]], inc=sB["dve"])
        p.op("vector", lambda e:
             e.tensor_scalar(gst[:, 5:6], gst[:, 4:5], 1e-8, None,
                             op0=ALU.max), waits=[sB["dve"]], inc=sB["dve"])
        p.op("vector", lambda e:
             e.reciprocal(gst[:, 6:7], gst[:, 5:6]),
             waits=[sB["dve"]], inc=sB["dve"])
        p.op("vector", lambda e:
             e.tensor_scalar(g16[:, :], g16[:, :], gst[:, 6:7], None,
                             op0=ALU.mult), waits=[sB["dve"]], inc=sB["dve"])
        p.op("tensor", lambda e:
             e.transpose(gtps[0:16, 0:128], g16[:, :], idf[:, :]),
             waits=[sB["dve"], sB["act"]], inc=sB["tr"])
        p.op("scalar", lambda e, tt=tt:
             e.activation(gT_bf[:, tt * 128:(tt + 1) * 128],
                          gtps[0:16, 0:128], AF.Copy),
             waits=[sB["tr"]], inc=sB["act"])

    p.fence("fB", [sB[k] for k in sB])
    scB.close()

    acc = p.sb("acc", [128, NDT * 1024], F32)

    y_st = [p.sem(f"sy{i}") for i in range(2)]
    sems_C = {k: p.sem("sC_" + k) for k in
              ("w1ld", "w1c", "w2ld", "w2c", "pe", "relu", "gate", "accadd",
               "grow", "gpl", "uld")}
    sems_F = {k: p.sem("sF_" + k) for k in
              ("tr", "trev", "xld", "dve", "act")}
    for half in range(2):
        scW = p.scope()

        def sbW(name, shape, dtype):
            return scW.enter_context(nc.sbuf_tensor(name + str(half),
                                                    shape, dtype))

        uTh = sbW("uTh", [128, NDT * 1024], BF16)
        h_all = sbW("h_all", [128, NFT * 1024], BF16)
        w1_st = [sbW("w1st", [128, D], F32)] * 2
        w1_bf = [sbW(f"w1bf{i}", [128, D], BF16) for i in range(2)]
        w2_st = [sbW("w2st", [128, NFT * 512], F32)] * 2
        w2_bf = [sbW(f"w2bf{i}", [128, NFT * 512], BF16) for i in range(2)]
        g_pl = sbW("g_pl", [128, 1024], BF16)
        grow = sbW("grow", [1, 1024], BF16)
        h_ps = [scW.enter_context(nc.psum_tensor(f"hps{i}{half}",
                                                 [128, 1024], F32))
                for i in range(2)]
        y_ps = [scW.enter_context(nc.psum_tensor(f"yps{i}{half}",
                                                 [128, 512], F32))
                for i in range(4)]
        sC = sems_C

        w1use, w2use, yuse = {}, {}, {}
        p.dma("gpsimd", fm3(uTh[:, :], 1024),
              uT_scr[:, half * 1024:(half + 1) * 1024]
              .rearrange("(dt p) t -> p dt t", p=128), sC["uld"])
        w1_jobs = []
        w2_jobs = []
        for e_i in range(NE):
            w1_jobs += [(e_i, f) for f in range(NFT)]
            w2_jobs += [(e_i, dg) for dg in range(4)]

        def w1_fetch(i):
            if i >= len(w1_jobs):
                return
            e_j, f_j = w1_jobs[i]
            wsrc = (rw1[e_j] if e_j < 16 else sw1[e_j - 16])
            wp = i % 2
            st3 = fm3(w1_st[0][:, :], 128)
            p.dma("gpsimd", st3[:, 0:NDT // 2, :], wsrc[f_j][:, 0:NDT // 2, :],
                  sC["w1ld"], waits=[sC["w1c"]])
            p.dma("scalar", st3[:, NDT // 2:, :], wsrc[f_j][:, NDT // 2:, :],
                  sC["w1ld"], waits=[sC["w1c"]])
            p.op("vector", lambda e, wp=wp:
                 e.tensor_copy(w1_bf[wp][:, :], w1_st[0][:, :]),
                 waits=[sC["w1ld"], (sC["pe"], w1use.get(wp, 0))],
                 inc=sC["w1c"])

        def w2_fetch(i):
            if i >= len(w2_jobs):
                return
            e_j, dg_j = w2_jobs[i]
            wsrc = (rw2[e_j] if e_j < 16 else sw2[e_j - 16])
            wp2 = i % 2
            st3 = fm3(w2_st[0][:, :], 512)
            p.dma("gpsimd", st3[:, 0:6, :], wsrc[dg_j][:, 0:6, :],
                  sC["w2ld"], waits=[sC["w2c"]])
            p.dma("scalar", st3[:, 6:, :], wsrc[dg_j][:, 6:, :],
                  sC["w2ld"], waits=[sC["w2c"]])
            p.op("vector", lambda e, wp2=wp2:
                 e.tensor_copy(w2_bf[wp2][:, :], w2_st[0][:, :]),
                 waits=[sC["w2ld"], (sC["pe"], w2use.get(wp2, 0))],
                 inc=sC["w2c"])

        w1_fetch(0)
        w2_fetch(0)
        w1c_counts = {}
        w2c_counts = {}
        w1c_counts[0] = sC["w1c"].n
        w2c_counts[0] = sC["w2c"].n
        w1_i = [1]
        w2_i = [1]

        for e_i in range(NE):
            routed = e_i < 16
            hp = e_i % 2
            if routed:
                p.dma("sync", grow[:, :],
                      gT_bf[e_i:e_i + 1, half * 1024:(half + 1) * 1024],
                      sC["grow"], waits=[sC["gpl"]])
                for q2 in range(2):
                    p.op("tensor", lambda e, hp=hp, q2=q2:
                         e.matmul(h_ps[hp][:, q2 * 512:(q2 + 1) * 512],
                                  ones_bsb[:, :],
                                  grow[:, q2 * 512:(q2 + 1) * 512],
                                  start=True, stop=True),
                         waits=([sC["grow"], sC["relu"], sC["gate"]]
                                if q2 == 0 else ()),
                         inc=(sC["pe"] if q2 == 1 else None))
                p.op("scalar", lambda e, hp=hp:
                     e.activation(g_pl[:, :], h_ps[hp][:, :], AF.Copy),
                     waits=[sC["pe"], sC["gate"]], inc=sC["gpl"])
            for f in range(NFT):
                wi = e_i * NFT + f
                wp = wi % 2
                w1_fetch(w1_i[0])
                w1c_counts[w1_i[0]] = sC["w1c"].n
                w1_i[0] += 1
                for q2 in range(2):
                    for dt in range(NDT):
                        p.op("tensor", lambda e, hp=hp, wp=wp, q2=q2, dt=dt:
                             e.matmul(h_ps[hp][:, q2 * 512:(q2 + 1) * 512],
                                      w1_bf[wp][:, dt * 128:(dt + 1) * 128],
                                      uTh[:, dt * 1024 + q2 * 512:
                                          dt * 1024 + (q2 + 1) * 512],
                                      start=(dt == 0), stop=(dt == NDT - 1)),
                             waits=([(sC["w1c"], w1c_counts[wi]),
                                     sC["relu"], sC["gate"],
                                     sC["gpl"], sC["uld"]] if dt == 0
                                    else ()),
                             inc=(sC["pe"] if dt == NDT - 1 else None))
                w1use[wp] = sC["pe"].n
                bcol = b1T[:, f * NE + e_i:f * NE + e_i + 1]
                hsl = h_all[:, f * 1024:(f + 1) * 1024]
                p.op("scalar", lambda e, hsl=hsl, hp=hp, bcol=bcol:
                     e.activation(hsl, h_ps[hp][:, :], AF.Relu, bias=bcol),
                     waits=[sC["pe"], sC["gate"]], inc=sC["relu"])
                if routed:
                    p.op("vector", lambda e, hsl=hsl:
                         e.tensor_tensor(hsl, hsl, g_pl[:, :], op=ALU.mult),
                         waits=[sC["relu"], sC["gpl"]], inc=sC["gate"])
            for dg in range(4):
                wi2 = e_i * 4 + dg
                wp2 = wi2 % 2
                w2_fetch(w2_i[0])
                w2c_counts[w2_i[0]] = sC["w2c"].n
                w2_i[0] += 1
                for q2 in range(2):
                    for dh in range(2):
                        yset = (wi2 * 4 + q2 * 2 + dh) % 2
                        ybase = yset * 2
                        for f in range(NFT):
                            for di2 in range(2):
                                di = dh * 2 + di2
                                p.op("tensor",
                                     lambda e, wp2=wp2, f=f, di=di, q2=q2,
                                     ybase=ybase, di2=di2:
                                     e.matmul(y_ps[ybase + di2][:, :],
                                              w2_bf[wp2][:, f * 512 + di * 128:
                                                         f * 512 +
                                                         (di + 1) * 128],
                                              h_all[:, f * 1024 + q2 * 512:
                                                    f * 1024 + (q2 + 1) * 512],
                                              start=(f == 0),
                                              stop=(f == NFT - 1)),
                                     waits=([(sC["w2c"], w2c_counts[wi2]),
                                             sC["gate"] if routed
                                             else sC["relu"],
                                             (sC["accadd"],
                                              yuse.get(yset, 0))]
                                            if f == 0 else ()),
                                     inc=(sC["pe"] if (f == NFT - 1 and
                                                       di2 == 1) else None))
                        for di2 in range(2):
                            di = dh * 2 + di2
                            dcol = (dg * 4 + di) * 1024 + q2 * 512
                            if e_i == 0:
                                p.op("vector", lambda e, di2=di2, dcol=dcol,
                                     ybase=ybase:
                                     e.tensor_copy(acc[:, dcol:dcol + 512],
                                                   y_ps[ybase + di2][:, :]),
                                     waits=[sC["pe"]], inc=sC["accadd"])
                            else:
                                p.op("vector", lambda e, di2=di2, dcol=dcol,
                                     ybase=ybase:
                                     e.tensor_tensor(acc[:, dcol:dcol + 512],
                                                     acc[:, dcol:dcol + 512],
                                                     y_ps[ybase + di2][:, :],
                                                     op=ALU.add),
                                     waits=[sC["pe"]], inc=sC["accadd"])
                        yuse[yset] = sC["accadd"].n
        for dt in range(NDT):
            for q2 in range(2):
                p.op("tensor", lambda e, dt=dt, q2=q2:
                     e.matmul(y_ps[0][:, :], b2r_bf[:, dt * 128:(dt + 1) * 128],
                              gT_bf[:, half * 1024 + q2 * 512:
                                    half * 1024 + (q2 + 1) * 512],
                              start=True, stop=False),
                     waits=[sC["accadd"]], inc=None)
                p.op("tensor", lambda e, dt=dt, q2=q2:
                     e.matmul(y_ps[0][:, :], b2s_bf[:, dt * 128:(dt + 1) * 128],
                              onesK2[:, :], start=False, stop=True),
                     inc=sC["pe"])
                p.op("vector", lambda e, dt=dt, q2=q2:
                     e.tensor_tensor(acc[:, dt * 1024 + q2 * 512:
                                         dt * 1024 + (q2 + 1) * 512],
                                     acc[:, dt * 1024 + q2 * 512:
                                         dt * 1024 + (q2 + 1) * 512],
                                     y_ps[0][:, :], op=ALU.add),
                     waits=[sC["pe"]], inc=sC["accadd"])
        p.fence(f"fC{half}", [sC[k] for k in sC])
        scW.close()

        scF = p.scope()

        def sbF(name, shape, dtype):
            return scF.enter_context(nc.sbuf_tensor(name + str(half),
                                                    shape, dtype))

        f_tm = [sbF(f"ftm{i}", [128, D], F32) for i in range(2)]
        ux_t = [sbF(f"uxt{i}", [128, D], F32) for i in range(2)]
        xx_t = [sbF(f"xxt{i}", [128, D], F32) for i in range(2)]
        ln2 = sbF("ln2", [128, D], F32)
        y_t = sbF("y_t", [128, D], F32)
        ftps = [scF.enter_context(nc.psum_tensor(f"ftps{i}{half}",
                                                 [128, 128], F32))
                for i in range(2)]
        sF = sems_F

        ftps_use, ftm_use, ln2_use, yt_use = {}, {}, [0], [0]
        for tt in range(8):
            fp = tt % 2
            p.dma("gpsimd", xx_t[fp][:, :],
                  x1_scr[half * 1024 + tt * 128: half * 1024 + (tt + 1) * 128,
                         :], sF["xld"], waits=[(sF["dve"],
                                                ftm_use.get(fp, 0))])
            p.op("vector", lambda e: e.nop(), waits=[sF["xld"]])
            p.op("scalar", lambda e: e.nop(), waits=[sF["xld"]])
            emit_ln(p, xx_t[fp][:, :], ux_t[fp][:, :], stat[fp][:, :],
                    sq_scr[:, :], sF["dve"], sF["act"], eps_col[:, :])
            for dt in range(NDT):
                tp = (tt * NDT + dt) % 2
                p.op("tensor", lambda e, tp=tp, dt=dt, tt=tt:
                     e.transpose(ftps[tp][:, :],
                                 acc[:, dt * 1024 + tt * 128:
                                     dt * 1024 + (tt + 1) * 128],
                                 idf[:, :]),
                     waits=[y_st[half], (sF["trev"], ftps_use.get(tp, 0))],
                     inc=sF["tr"])
                p.op("scalar", lambda e, tp=tp, dt=dt, fp=fp:
                     e.activation(f_tm[fp][:, dt * 128:(dt + 1) * 128],
                                  ftps[tp][:, :], AF.Copy),
                     waits=[sF["tr"], (sF["dve"], ftm_use.get(fp, 0))],
                     inc=sF["trev"])
                ftps_use[tp] = sF["trev"].n
            p.op("vector", lambda e, fp=fp:
                 e.tensor_tensor(f_tm[fp][:, :], f_tm[fp][:, :],
                                 ux_t[fp][:, :], op=ALU.add),
                 waits=[sF["trev"], sF["dve"]], inc=sF["dve"])
            p.op("scalar", lambda e: e.nop(), waits=[sF["dve"]])
            emit_ln(p, f_tm[fp][:, :], ln2[:, :], stat[2 + fp][:, :],
                    sq_scr[:, :], sF["dve"], sF["act"], eps_col[:, :])
            ftm_use[fp] = sF["dve"].n
            p.op("vector", lambda e, fp=fp:
                 e.tensor_tensor(y_t[:, :], xx_t[fp][:, :], ln2[:, :],
                                 op=ALU.add),
                 waits=[sF["dve"], sF["xld"], y_st[half]], inc=sF["dve"])
            p.dma("scalar",
                  y[half * 1024 + tt * 128: half * 1024 + (tt + 1) * 128, :],
                  y_t[:, :], y_st[half], waits=[sF["dve"]])
        p.fence(f"fF{half}", list(sF.values()) + [y_st[half]])
        scF.close()

    p.op("gpsimd", lambda e: e.nop(), waits=[y_st[0], y_st[1]])


def build_l3():
    nc = bass.Bass("TRN2", target_bir_lowering=False)
    x = nc.dram_tensor("x", [TPC, D], F32, kind="ExternalInput").ap()
    oT = nc.dram_tensor("oT", [D, TPC], BF16, kind="ExternalInput").ap()
    woutr = nc.dram_tensor("woutr", [NDT, 128, NDT, 128], F32,
                           kind="ExternalInput").ap()
    sw1 = nc.dram_tensor("sw1", [2, NFT, 128, NDT, 128], F32,
                         kind="ExternalInput").ap()
    sb1 = nc.dram_tensor("sb1", [2, DFF], F32, kind="ExternalInput").ap()
    sw2 = nc.dram_tensor("sw2", [2, 4, 128, NFT, 512], F32,
                         kind="ExternalInput").ap()
    sb2 = nc.dram_tensor("sb2", [2, D], F32, kind="ExternalInput").ap()
    rw1 = nc.dram_tensor("rw1", [16, NFT, 128, NDT, 128], F32,
                         kind="ExternalInput").ap()
    rb1 = nc.dram_tensor("rb1", [16, DFF], F32, kind="ExternalInput").ap()
    rw2 = nc.dram_tensor("rw2", [16, 4, 128, NFT, 512], F32,
                         kind="ExternalInput").ap()
    rb2 = nc.dram_tensor("rb2", [16, D], F32, kind="ExternalInput").ap()
    cent = nc.dram_tensor("cent", [16, D], F32, kind="ExternalInput").ap()
    cbias = nc.dram_tensor("cbias", [1, 16], F32, kind="ExternalInput").ap()
    y = nc.dram_tensor("y", [TPC, D], F32, kind="ExternalOutput").ap()
    x1_scr = nc.dram_tensor("x1_scr", [TPC, D], F32, kind="Internal").ap()
    uT_scr = nc.dram_tensor("uT_scr", [D, TPC], BF16, kind="Internal").ap()
    ident_bf = nc.inline_tensor(np.eye(128, dtype=BF16NP), name="id_bf")
    ident_f = nc.inline_tensor(np.eye(128, dtype=np.float32), name="id_f")
    ones_r = nc.inline_tensor(np.ones((1, 128), np.float32), name="ones_r")
    ones_rb = nc.inline_tensor(np.ones((1, 128), dtype=BF16NP),
                               name="ones_rb")
    p = Prog(nc)
    emit_l3_body(p, nc, x, oT, woutr, sw1, sb1, sw2, sb2, rw1, rb1, rw2,
                 rb2, cent, cbias, y, x1_scr, uT_scr, ident_bf, ident_f,
                 ones_r, ones_rb)
    p.emit()
    p.close()
    return nc


def build_fused():
    nc = bass.Bass("TRN2", target_bir_lowering=False)
    xb = nc.dram_tensor("xb", [S, D], F32, kind="ExternalInput").ap()
    x = nc.dram_tensor("x", [TPC, D], F32, kind="ExternalInput").ap()
    wqkvr = nc.dram_tensor("wqkvr", [2 * NDT, 128, NDT, 128], F32,
                           kind="ExternalInput").ap()
    wvr = nc.dram_tensor("wvr", [128, NDT, D], F32,
                         kind="ExternalInput").ap()
    woutr = nc.dram_tensor("woutr", [NDT, 128, NDT, 128], F32,
                           kind="ExternalInput").ap()
    sw1 = nc.dram_tensor("sw1", [2, NFT, 128, NDT, 128], F32,
                         kind="ExternalInput").ap()
    sb1 = nc.dram_tensor("sb1", [2, DFF], F32, kind="ExternalInput").ap()
    sw2 = nc.dram_tensor("sw2", [2, 4, 128, NFT, 512], F32,
                         kind="ExternalInput").ap()
    sb2 = nc.dram_tensor("sb2", [2, D], F32, kind="ExternalInput").ap()
    rw1 = nc.dram_tensor("rw1", [16, NFT, 128, NDT, 128], F32,
                         kind="ExternalInput").ap()
    rb1 = nc.dram_tensor("rb1", [16, DFF], F32, kind="ExternalInput").ap()
    rw2 = nc.dram_tensor("rw2", [16, 4, 128, NFT, 512], F32,
                         kind="ExternalInput").ap()
    rb2 = nc.dram_tensor("rb2", [16, D], F32, kind="ExternalInput").ap()
    cent = nc.dram_tensor("cent", [16, D], F32, kind="ExternalInput").ap()
    cbias = nc.dram_tensor("cbias", [1, 16], F32, kind="ExternalInput").ap()
    y = nc.dram_tensor("y", [TPC, D], F32, kind="ExternalOutput").ap()

    qT_scr = nc.dram_tensor("qT_scr", [D, TPC], BF16, kind="Internal").ap()
    wqkb_scr = nc.dram_tensor("wqkb_scr", [2 * NDT, 128, NDT * 128], BF16,
                              kind="Internal").ap()
    kT_scr = nc.dram_tensor("kT_scr", [D, S], BF16, kind="Internal").ap()
    v_scr = nc.dram_tensor("v_scr", [S, D], BF16, kind="Internal").ap()
    oT_scr = nc.dram_tensor("oT_scr", [D, TPC], BF16, kind="Internal").ap()
    x1_scr = nc.dram_tensor("x1_scr", [TPC, D], F32, kind="Internal").ap()
    uT_scr = nc.dram_tensor("uT_scr", [D, TPC], BF16, kind="Internal").ap()

    ident_bf = nc.inline_tensor(np.eye(128, dtype=BF16NP), name="id_bf")
    ident_f = nc.inline_tensor(np.eye(128, dtype=np.float32), name="id_f")
    ones_r = nc.inline_tensor(np.ones((1, 128), np.float32), name="ones_r")
    ones_rb = nc.inline_tensor(np.ones((1, 128), dtype=BF16NP),
                               name="ones_rb")
    masks = nc.inline_tensor(causal_mask_np(), name="cmasks")

    p = Prog(nc)

    sc0 = p.scope()

    def a0(name, shape, dtype):
        return sc0.enter_context(nc.sbuf_tensor(name, shape, dtype))

    def ps0(name, shape, dtype=F32):
        return sc0.enter_context(nc.psum_tensor(name, shape, dtype))

    idb0 = a0("idb0", [128, 128], BF16)
    s_id0 = p.sem("q_id0")
    p.dma("gpsimd", idb0[:, :], ident_bf.ap(), s_id0)
    p.op("tensor", lambda e: e.nop(), waits=[(s_id0, 16)])
    c = qkv_ctx(p, a0, ps0, idb=idb0)
    emit_wv_cast(p, c, wvr)
    s_wpc = p.sem("q_wpc")
    for ct in range(2 * NDT):
        par = ct % 2
        p.dma("gpsimd", fm3(c["wst"][par][:, :], 128), wqkvr[ct],
              c["s_wld%d" % par], waits=[c["s_wqk%d" % par]])
        p.op("vector", lambda e, par=par:
             e.tensor_copy(c["wqk_bf"][par][:, :], c["wst"][par][:, :]),
             waits=[c["s_wld%d" % par], (s_wpc, 0 if ct < 2 else
                                         (ct - 2) * 16 + 16)],
             inc=c["s_wqk%d" % par])
        p.dma("scalar", wqkb_scr[ct], c["wqk_bf"][par][:, :], s_wpc,
              waits=[c["s_wqk%d" % par]])
    c["wquse"][0] = c["s_pe"].n
    c["wquse"][1] = c["s_pe"].n
    p.op("gpsimd", lambda e: e.nop(), waits=[(s_wpc, 2 * NDT * 16)])
    for sb_ in range(2):
        xb_ = sb_ % 2
        emit_xnT(p, c, x[sb_ * 1024:(sb_ + 1) * 1024, :], 1024, xb_)
        emit_qk(p, c, wqkb_scr, list(range(NDT)), qT_scr, 0, sb_ * 1024,
                1024, QSCALE, xb_, bf16_src=True)
    for tb in range(8):
        xb_ = tb % 2
        emit_xnT(p, c, xb[tb * 1024:(tb + 1) * 1024, :], 1024, xb_)
        emit_qk(p, c, wqkb_scr, list(range(NDT, 2 * NDT)), kT_scr, D,
                tb * 1024, 1024, 1.0, xb_, bf16_src=True)
        emit_v(p, c, v_scr, tb * 1024, 1024, xb_)
    f0 = p.fence("f_p0", [c["s_st0"], c["s_st1"], c["s_pe"], c["s_dve"],
                          c["s_act"], c["s_tr"], c["s_wv"]])
    sc0.close()

    sc1 = p.scope()

    def a1(name, shape, dtype):
        return sc1.enter_context(nc.sbuf_tensor(name, shape, dtype))

    common = {
        "q": [a1(f"aqb{i}", [128, 512], BF16) for i in range(3)],
        "k": [a1(f"akb{i}", [128, 512], BF16) for i in range(4)],
        "v": [a1(f"avb{i}", [128, 4 * 129], BF16) for i in range(4)],
        "pT": [a1(f"apT{i}", [128, 512], BF16) for i in range(3)],
        "o_ev": [a1(f"aoe{i}", [128, 128], BF16) for i in range(2)],
        "tr": [a1(f"atr{i}", [128, 128], BF16) for i in range(2)],
        "l_r": a1("alr", [128, 4], F32),
        "masks": a1("amasks", [128, 4 * 512], BF16),
        "st": [sc1.enter_context(nc.psum_tensor(f"ast{i}", [128, 512],
                                                 F32)) for i in range(3)],
        "oacc": [sc1.enter_context(nc.psum_tensor(f"aoa{i}", [128, 129],
                                                  F32)) for i in range(4)],
    }
    s_mset = p.sem("a_mset")
    p.dma("gpsimd", common["masks"][:, :], masks.ap(), s_mset)
    for i in range(4):
        for t4 in range(4):
            p.op("vector", lambda e, i=i, t4=t4:
                 e.memset(common["v"][i][:, t4 * 129 + 128:
                                         t4 * 129 + 129], 1.0),
                 inc=s_mset, k=1)

    p1done = p.sem("p1done")
    sems = {k: p.sem("ap_" + k) for k in
            ("pe", "exp", "msk", "dve", "oev", "tr", "ost")}
    sems["q"] = [p.sem(f"ap_q{i}") for i in range(3)]
    sems["k"] = [p.sem(f"ap_k{i}") for i in range(4)]
    sems["v"] = [p.sem(f"ap_v{i}") for i in range(4)]
    path_sems = ([sems[k] for k in
                  ("pe", "exp", "msk", "dve", "oev", "tr", "ost")]
                 + sems["q"] + sems["k"] + sems["v"])
    snap = {id(s): s.n for s in path_sems}
    for j in range(4):
        p.set_path(j)
        for s_ in path_sems:
            s_.n = snap[id(s_)]
        bufs = dict(common)
        p.op("tensor", lambda e: e.nop(), waits=[(s_mset, s_mset.n)])
        for h in range(H):
            for ci_ch, ch in enumerate((j, 7 - j)):
                emit_attn_pair(
                    p, bufs, sems,
                    qT_scr[h * DH:(h + 1) * DH, :],
                    kT_scr[h * DH:(h + 1) * DH, :],
                    v_scr[:, h * DH:(h + 1) * DH],
                    oT_scr[h * DH:(h + 1) * DH,
                           ci_ch * 1024:(ci_ch + 1) * 1024],
                    chunks=[2 * ch, 2 * ch + 1],
                    qcols=[2 * ci_ch, 2 * ci_ch + 1])
        p.op("sync", lambda e: e.nop(),
             waits=[sems["ost"], sems["tr"]])
        p.op("gpsimd", lambda e: e.nop(),
             waits=[(sems["k"][i], sems["k"][i].n) for i in range(4)] +
                   [(sems["q"][i], sems["q"][i].n) for i in range(3)])
        p.op("scalar", lambda e: e.nop(),
             waits=[(sems["v"][i], sems["v"][i].n) for i in range(4)])
        for eng in ENGS:
            self_inc = (lambda e: e.sem_inc(p1done.h, 1))
            p.op(eng, self_inc)
    p.set_path(None)
    p1done.n = 5
    for eng in ENGS:
        p.op(eng, lambda e: e.nop(), waits=[(p1done, 5)])
    sc1.close()

    emit_l3_body(p, nc, x, oT_scr, woutr, sw1, sb1, sw2, sb2, rw1, rb1,
                 rw2, rb2, cent, cbias, y, x1_scr, uT_scr, ident_bf,
                 ident_f, ones_r, ones_rb)
    p.emit()
    p.close()
    return nc


def _chunk_rows(j):
    a = np.arange(j * 1024, (j + 1) * 1024)
    b_ = np.arange((7 - j) * 1024, (8 - j) * 1024)
    return np.concatenate([a, b_])


_NC_CACHE = {}


def _get_nc(name):
    if name not in _NC_CACHE:
        _NC_CACHE[name] = {"l1": build_l1, "l2": build_l2,
                           "l3": build_l3, "fused": build_fused}[name]()
    return _NC_CACHE[name]


def _relay_w1(w1):
    e = w1.shape[0]
    return np.ascontiguousarray(
        w1.reshape(e, NDT, 128, NFT, 128).transpose(0, 3, 2, 1, 4))


def _relay_w2(w2):
    e = w2.shape[0]
    return np.ascontiguousarray(
        w2.reshape(e, NFT, 128, 4, 512).transpose(0, 3, 2, 1, 4))


def _relay_sq(w):
    c = w.shape[1]
    return np.ascontiguousarray(
        w.reshape(NDT, 128, c // 128, 128).transpose(2, 1, 0, 3))


USE_FUSED = True


def kernel(L0_attn_x, L0_attn_w_qkv, L0_attn_w_out,
           L0_ffn_shared_w1, L0_ffn_shared_b1, L0_ffn_shared_w2,
           L0_ffn_shared_b2, L0_ffn_routed_w1, L0_ffn_routed_b1,
           L0_ffn_routed_w2, L0_ffn_routed_b2, L0_ffn_centroids,
           L0_ffn_bias):
    x = np.ascontiguousarray(np.asarray(L0_attn_x, dtype=np.float32))
    wqkv = np.ascontiguousarray(np.asarray(L0_attn_w_qkv, dtype=np.float32))

    rows = [_chunk_rows(c % 4) for c in range(NCORES)]

    if USE_FUSED:
        wqkvr = _relay_sq(wqkv[:, :2 * D])
        wvr = np.ascontiguousarray(
            wqkv[:, 2 * D:].reshape(NDT, 128, D).transpose(1, 0, 2))
        woutr = _relay_sq(np.asarray(L0_attn_w_out, np.float32))
        sw1r = _relay_w1(np.asarray(L0_ffn_shared_w1, np.float32))
        sw2r = _relay_w2(np.asarray(L0_ffn_shared_w2, np.float32))
        rw1r = _relay_w1(np.asarray(L0_ffn_routed_w1, np.float32))
        rw2r = _relay_w2(np.asarray(L0_ffn_routed_w2, np.float32))
        ins = []
        for c in range(NCORES):
            b_ = c // 4
            ins.append({
                "xb": x[b_],
                "x": np.ascontiguousarray(x[b_][rows[c]]),
                "wqkvr": wqkvr, "wvr": wvr, "woutr": woutr,
                "sw1": sw1r, "sw2": sw2r, "rw1": rw1r, "rw2": rw2r,
                "sb1": np.ascontiguousarray(
                    np.asarray(L0_ffn_shared_b1, np.float32)),
                "sb2": np.ascontiguousarray(
                    np.asarray(L0_ffn_shared_b2, np.float32)),
                "rb1": np.ascontiguousarray(
                    np.asarray(L0_ffn_routed_b1, np.float32)),
                "rb2": np.ascontiguousarray(
                    np.asarray(L0_ffn_routed_b2, np.float32)),
                "cent": np.ascontiguousarray(
                    np.asarray(L0_ffn_centroids, np.float32)),
                "cbias": np.ascontiguousarray(
                    np.asarray(L0_ffn_bias, np.float32).reshape(1, 16)),
            })
        res = run_bass_kernel_spmd(_get_nc("fused"), ins,
                                   core_ids=list(range(NCORES)))
        out = np.empty((B, S, D), dtype=np.float32)
        for c in range(NCORES):
            out[c // 4][rows[c]] = res.results[c]["y"]
        return out

    wqkvr = _relay_sq(wqkv[:, :2 * D])
    wvr = np.ascontiguousarray(
        wqkv[:, 2 * D:].reshape(NDT, 128, D).transpose(1, 0, 2))
    in1 = [{"x": np.ascontiguousarray(x[c // 4][rows[c]]),
            "wqkvr": wqkvr, "wvr": wvr}
           for c in range(NCORES)]
    r1 = run_bass_kernel_spmd(_get_nc("l1"), in1, core_ids=list(range(NCORES)))
    qTg = np.empty((B, D, S), dtype=BF16NP)
    kTg = np.empty((B, D, S), dtype=BF16NP)
    vg = np.empty((B, S, D), dtype=BF16NP)
    for c in range(NCORES):
        b_ = c // 4
        qTg[b_][:, rows[c]] = r1.results[c]["qT"]
        kTg[b_][:, rows[c]] = r1.results[c]["kT"]
        vg[b_][rows[c]] = r1.results[c]["v"]

    in2 = []
    for c in range(NCORES):
        m = {}
        for i in range(4):
            pair = 4 * c + i
            b_, h_ = pair // H, pair % H
            m[f"q{i}"] = np.ascontiguousarray(qTg[b_, h_ * DH:(h_ + 1) * DH])
            m[f"k{i}"] = np.ascontiguousarray(kTg[b_, h_ * DH:(h_ + 1) * DH])
            m[f"v{i}"] = np.ascontiguousarray(vg[b_, :, h_ * DH:(h_ + 1) * DH])
        in2.append(m)
    r2 = run_bass_kernel_spmd(_get_nc("l2"), in2, core_ids=list(range(NCORES)))
    oTg = np.empty((B, D, S), dtype=BF16NP)
    for c in range(NCORES):
        for i in range(4):
            pair = 4 * c + i
            b_, h_ = pair // H, pair % H
            oTg[b_, h_ * DH:(h_ + 1) * DH] = r2.results[c][f"o{i}"]

    woutr = _relay_sq(np.asarray(L0_attn_w_out, np.float32))
    sw1r = _relay_w1(np.asarray(L0_ffn_shared_w1, np.float32))
    sw2r = _relay_w2(np.asarray(L0_ffn_shared_w2, np.float32))
    rw1r = _relay_w1(np.asarray(L0_ffn_routed_w1, np.float32))
    rw2r = _relay_w2(np.asarray(L0_ffn_routed_w2, np.float32))
    in3 = []
    for c in range(NCORES):
        b_ = c // 4
        in3.append({
            "x": np.ascontiguousarray(x[b_][rows[c]]),
            "oT": np.ascontiguousarray(oTg[b_][:, rows[c]]),
            "woutr": woutr,
            "sw1": sw1r,
            "sb1": np.ascontiguousarray(np.asarray(L0_ffn_shared_b1,
                                                   np.float32)),
            "sw2": sw2r,
            "sb2": np.ascontiguousarray(np.asarray(L0_ffn_shared_b2,
                                                   np.float32)),
            "rw1": rw1r,
            "rb1": np.ascontiguousarray(np.asarray(L0_ffn_routed_b1,
                                                   np.float32)),
            "rw2": rw2r,
            "rb2": np.ascontiguousarray(np.asarray(L0_ffn_routed_b2,
                                                   np.float32)),
            "cent": np.ascontiguousarray(np.asarray(L0_ffn_centroids,
                                                    np.float32)),
            "cbias": np.ascontiguousarray(
                np.asarray(L0_ffn_bias, np.float32).reshape(1, 16)),
        })
    r3 = run_bass_kernel_spmd(_get_nc("l3"), in3, core_ids=list(range(NCORES)))

    out = np.empty((B, S, D), dtype=np.float32)
    for c in range(NCORES):
        out[c // 4][rows[c]] = r3.results[c]["y"]
    return out
